# version 02 (final)
# baseline (speedup 1.0000x reference)
import contextlib
import numpy as np
import ml_dtypes
import concourse.bass as bass
import concourse.mybir as mybir
from concourse.bass_utils import run_bass_kernel_spmd

F32 = mybir.dt.float32
BF16 = mybir.dt.bfloat16
I32 = mybir.dt.int32
U32 = mybir.dt.uint32
AX = mybir.AxisListType
ALU = mybir.AluOpType
ACTF = mybir.ActivationFunctionType

B, T, D = 4, 4096, 1024
TC = 2048
NCH = D // 128
HALO = 1024
NEG = -30000.0
import os as _os00
N_STAGES = int(_os00.environ.get("BASSNST", "4"))
S2LVL = int(_os00.environ.get("BASSS2", "5"))
import os as _os0
DBG_BARRIER = _os0.environ.get("BASSBAR", "") == "1"
DBG_DUMP = _os0.environ.get("BASSDBG", "") == "1"
DBG_SINGLEBUF = _os0.environ.get("BASSSB", "") == "1"
FORCE1 = set(_os0.environ.get("BASSF1", "").split(","))
FORCE0 = set(_os0.environ.get("BASSF0", "").split(","))
SWAPALLOC = _os0.environ.get("BASSSWAP", "") == "1"
DBG_DRAIN = _os0.environ.get("BASSDRAIN", "") == "1"
DBG_SERIAL = _os0.environ.get("BASSSER", "") == "1"
ALIGN512 = _os0.environ.get("BASSALIGN", "") == "1"


ENGS = ("sync", "scalar", "vector", "tensor", "gpsimd")


class Buf:
    __slots__ = ("name", "writer", "readers")

    def __init__(self, name=""):
        self.name = name
        self.writer = None
        self.readers = []


class Prog:

    def __init__(self, nc):
        self.nc = nc
        self.ops = []
        self.global_deps = []
        self._chk_of = {}
        self._pending = {e: [] for e in ENGS}

    def _resolve_dep(self, d):
        op = self.ops[d]
        if op["kind"] == "dma":
            c = self._chk_of.get(d)
            if c is None:
                c = self._fence(op["engine"])
                assert self._chk_of.get(d) == c
            return c
        return d

    def _fence(self, q):
        pend = self._pending[q]
        assert pend, f"fence on empty queue {q}"
        i = len(self.ops)
        self.ops.append(dict(i=i, engine=q, emit=None, deps=[], kind="chk",
                             tag=f"chk_{q}", need_inc=False, event=None,
                             chk_count=None))
        for d in pend:
            self._chk_of[d] = i
        self._pending[q] = []
        return i

    def fence_all(self):
        for q in ENGS:
            if self._pending[q]:
                self._fence(q)

    def add(self, engine, emit, reads=(), writes=(), deps=(), kind="compute",
            tag=""):
        dep = set()
        for d in self.global_deps:
            dep.add(self._resolve_dep(d))
        for b in reads:
            if b.writer is not None:
                dep.add(self._resolve_dep(b.writer))
        for b in writes:
            for r in b.readers:
                dep.add(self._resolve_dep(r))
            if b.writer is not None:
                dep.add(self._resolve_dep(b.writer))
        for d in deps:
            if d is not None:
                dep.add(self._resolve_dep(d))
        if DBG_SERIAL and self.ops:
            j = len(self.ops) - 1
            dep.add(self._resolve_dep(j) if self.ops[j]["kind"] == "dma" else j)
        i = len(self.ops)
        dep.discard(i)
        self.ops.append(dict(i=i, engine=engine, emit=emit, deps=sorted(dep),
                             kind=kind, tag=tag, need_inc=False, event=None))
        if kind == "dma":
            self._pending[engine].append(i)
        for b in reads:
            b.readers.append(i)
        for b in writes:
            b.writer = i
            b.readers = []
        return i

    def dma(self, queue, out, in_, reads=(), writes=(), deps=(), tag=""):
        return self.add(queue, lambda eng: eng.dma_start(out=out, in_=in_),
                        reads=reads, writes=writes, deps=deps, kind="dma",
                        tag=tag)

    def barrier(self):
        self.fence_all()
        last = {}
        for op in self.ops:
            if op["kind"] == "dma":
                continue
            last[(op["engine"], op["kind"])] = op["i"]
        self.global_deps = sorted(set(last.values()))

    def build(self, sems):
        self.fence_all()
        for op in self.ops:
            for d in op["deps"]:
                dop = self.ops[d]
                assert dop["kind"] != "dma"
                if dop["kind"] in ("compute",) and dop["engine"] == op["engine"]:
                    continue
                if dop["kind"] == "chk" and dop["engine"] == op["engine"]:
                    continue
                dop["need_inc"] = True
        ccount = {e: 0 for e in ENGS}
        dcount = {e: 0 for e in ENGS}
        kcount = {e: 0 for e in ENGS}
        cc_n = 0
        for op in self.ops:
            e = op["engine"]
            if op["kind"] == "dma":
                dcount[e] += 1
                op["event"] = None
            elif op["kind"] == "chk":
                kcount[e] += 1
                op["chk_count"] = 16 * dcount[e]
                op["event"] = (sems["k_" + e], kcount[e])
            elif op["kind"] == "cc":
                cc_n += 1
                op["need_inc"] = True
                op["event"] = (sems["cc"], cc_n)
            elif op["need_inc"]:
                ccount[e] += 1
                op["event"] = (sems["c_" + e], ccount[e])
        self._streams = {e: [op for op in self.ops if op["engine"] == e]
                         for e in ENGS}
        for e, stream in self._streams.items():
            pos = {op["i"]: k for k, op in enumerate(stream)}
            for k, op in enumerate(stream):
                if op["kind"] != "compute":
                    continue
                for d in op["deps"]:
                    dop = self.ops[d]
                    if (dop["engine"] == e and dop["kind"] == "compute"
                            and k - pos.get(d, -100) <= 4):
                        op["predrain"] = True
                        break
        self._sems = sems

    def emit_engine(self, ename, eng):
        sems = self._sems
        waited = {}
        for op in self._streams[ename]:
            for d in op["deps"]:
                dop = self.ops[d]
                if dop["engine"] == ename and dop["kind"] in ("compute", "chk"):
                    continue
                ev = dop["event"]
                if ev is None:
                    continue
                sem, val = ev
                if waited.get(id(sem), -1) >= val:
                    continue
                eng.wait_ge(sem, val)
                waited[id(sem)] = val
            if op["kind"] == "chk":
                eng.wait_ge(sems["d_" + ename], op["chk_count"])
                eng.sem_inc(sems["k_" + ename], 1)
                continue
            if op.get("predrain"):
                eng.drain()
            inst = op["emit"](eng)
            if DBG_DRAIN and ename in ("vector", "scalar") and op["kind"] == "compute":
                inst = eng.drain()
            if op["kind"] == "dma":
                inst.then_inc(sems["d_" + ename], 16)
            elif op["kind"] == "cc":
                inst.then_inc(sems["cc"], 1)
            elif op["need_inc"]:
                inst.then_inc(sems["c_" + ename], 1)

    def finish_events(self):
        evs = {}
        for op in self.ops:
            if op["event"] is not None:
                evs[id(op["event"][0])] = op["event"]
        return list(evs.values())


class Alloc:

    def __init__(self, nc, arena_cols):
        self.nc = nc
        self.arenas = []
        for i, cols in enumerate(arena_cols):
            self.arenas.append((nc.alloc_sbuf_tensor(f"AR{i}", [128, cols], F32),
                                cols))
        self.off = [0] * len(self.arenas)

    def mark(self):
        return list(self.off)

    def reset(self, mk):
        self.off = list(mk)

    def take(self, shape, dtype=F32, name=""):
        part = shape[0]
        free = int(np.prod(shape[1:]))
        dsz = mybir.dt.size(dtype)
        f32cols = free if dsz == 4 else (free * dsz + 3) // 4
        alloc_cols = f32cols
        if ALIGN512:
            alloc_cols = (f32cols + 255) // 256 * 256
        for ai, (tens, tot) in enumerate(self.arenas):
            if self.off[ai] + alloc_cols <= tot:
                off = self.off[ai]
                self.off[ai] = off + alloc_cols
                ap = tens[0:part, off:off + f32cols]
                if dtype != F32:
                    ap = ap.bitcast(dtype)
                if len(shape) > 2:
                    dims = " ".join(f"d{i}" for i in range(len(shape) - 1))
                    ap = ap.rearrange(f"p ({dims}) -> p {dims}",
                                      **{f"d{i}": shape[1 + i]
                                         for i in range(len(shape) - 1)})
                return ap, Buf(name)
        raise RuntimeError(f"SBUF exhausted: {name} {shape} {dtype}")



def rearr_cp(dram_ap):
    return dram_ap.rearrange("(c p) t -> p c t", p=128)


def build_stage1(P, al, io, ps, bps, pb, bpb):
    nc = P.nc
    KT = (HALO + TC) // 128
    QT = TC // 128

    mark0 = al.mark()
    WQKV, bWQKV = al.take((128, NCH, 1536), BF16, "WQKV")
    WQKR, bWQKR = al.take((128, NCH, 512), BF16, "WQKR")
    WVR, bWVR = al.take((128, NCH, 512), BF16, "WVR")
    WPRJ, bWPRJ = al.take((128, NCH, 1024), BF16, "WPRJ")
    VALID, bVALID = al.take((128, HALO), F32, "VALID")
    BMASK, bBMASK = al.take((128, 1152), F32, "BMASK")
    IDENT, bIDENT = al.take((128, 128), BF16, "IDENT")
    ONES1, bONES1 = al.take((128, 1), F32, "ONES1")
    ONESR, bONESR = al.take((128, 128), F32, "ONESR")

    lw = [P.dma("gpsimd", WQKV, rearr_cp(io["w_qkv"].ap()), writes=(bWQKV,)),
          P.dma("gpsimd", WQKR, rearr_cp(io["w_qkr"].ap()), writes=(bWQKR,)),
          P.dma("gpsimd", WVR, rearr_cp(io["w_vr"].ap()), writes=(bWVR,)),
          P.dma("gpsimd", WPRJ, rearr_cp(io["w_prj"].ap()), writes=(bWPRJ,))]
    P.dma("sync", BMASK, io["bmask"].ap(), writes=(bBMASK,))
    P.dma("sync", IDENT, io["identb"].ap(), writes=(bIDENT,))
    vld = P.dma("sync", VALID[0:1, :], io["valid"].ap(), writes=(bVALID,))
    P.add("vector", lambda eng: eng.memset(ONES1, 1.0), writes=(bONES1,))
    P.add("vector", lambda eng: eng.memset(ONESR[0:1, :], 1.0), writes=(bONESR,))
    for j in range(2):
        sl = slice(j * 512, (j + 1) * 512)
        P.add("tensor",
              lambda eng, sl=sl: eng.matmul(ps[0][:, 0:512], ONESR[0:1, :],
                                            VALID[0:1, sl], start=True, stop=True),
              reads=(bONESR, bVALID), writes=(bps[0],))
        P.add("vector",
              lambda eng, sl=sl: eng.tensor_copy(VALID[:, sl], ps[0][:, 0:512]),
              reads=(bps[0],), writes=(bVALID,))

    markA = al.mark()
    XW = [al.take((128, NCH, 512), BF16, f"XW{i}") for i in range(2)]
    kfd, bkfd = io["kfd"], Buf("kfd")
    krd, bkrd = io["krd"], Buf("krd")
    vfd, bvfd = io["vfd"], Buf("vfd")
    vrd, bvrd = io["vrd"], Buf("vrd")
    qfd, bqfd = io["qfd"], Buf("qfd")
    qrd, bqrd = io["qrd"], Buf("qrd")

    for w0 in range(0, HALO + TC, 512):
        par = (w0 // 512) % 2
        XWa, bXW = XW[par]
        if w0 < HALO:
            src = rearr_cp(io["xhT"].ap())[:, :, w0:w0 + 512]
        else:
            src = rearr_cp(io["xT"].ap())[:, :, w0 - HALO:w0 - HALO + 512]
        ld = P.add("gpsimd",
                   lambda eng, src=src, XWa=XWa: eng.dma_start(out=XWa, in_=src),
                   writes=(bXW,), kind="dma", tag=f"ldxw{w0}")
        main = w0 >= HALO

        def fproj(WAP, bW, wcol, OUTD, bOUTD, scale, tag):
            for c in range(NCH):
                P.add("tensor",
                      lambda eng, c=c, WAP=WAP, wcol=wcol, XWa=XWa: eng.matmul(
                          ps[0][:, 0:512], WAP[:, c, wcol:wcol + 128],
                          XWa[:, c, :], start=(c == 0), stop=(c == NCH - 1)),
                      reads=(bW, bXW), writes=(bps[0],), deps=(ld,), tag=tag + "mm")
            ev, bEV = al_ev[tag]
            if scale == 1.0:
                P.add("scalar",
                      lambda eng, ev=ev: eng.activation(ev, ps[0][:, 0:512], ACTF.Copy),
                      reads=(bps[0],), writes=(bEV,), tag=tag + "ev")
            else:
                P.add("vector",
                      lambda eng, ev=ev, scale=scale: eng.tensor_scalar_mul(
                          ev, ps[0][:, 0:512], scale),
                      reads=(bps[0],), writes=(bEV,), tag=tag + "ev")
            P.add("sync",
                  lambda eng, ev=ev, OUTD=OUTD, wcol=wcol: eng.dma_start(
                      out=OUTD, in_=ev),
                  reads=(bEV,), writes=(bOUTD,), kind="dma", tag=tag + "st")

        if w0 == 0:
            al_ev = {}
            for t in ["kf0", "kf1", "kf2", "kf3", "kr0", "kr1",
                      "qf0", "qf1", "qf2", "qf3", "qr0", "qr1", "vf", "vr"]:
                al_ev[t] = al.take((128, 512), BF16, "ev" + t)

        for m in range(4):
            fproj(WQKV, bWQKV, 512 + m * 128,
                  kfd.ap().rearrange("(c p) t -> p c t", p=128)[:, m, w0:w0 + 512],
                  bkfd, 1.0, f"kf{m}")
        for m in range(2):
            fproj(WQKR, bWQKR, 256 + m * 128,
                  krd.ap().rearrange("(c p) t -> p c t", p=128)[:, m, w0:w0 + 512],
                  bkrd, 1.0, f"kr{m}")
        if main:
            for m in range(4):
                fproj(WQKV, bWQKV, m * 128,
                      qfd.ap().rearrange("(c p) t -> p c t", p=128)[:, m, w0 - HALO:w0 - HALO + 512],
                      bqfd, 0.125, f"qf{m}")
            for m in range(2):
                fproj(WQKR, bWQKR, m * 128,
                      qrd.ap().rearrange("(c p) t -> p c t", p=128)[:, m, w0 - HALO:w0 - HALO + 512],
                      bqrd, 1.0 / np.sqrt(32.0), f"qr{m}")

        def vproj(WAP, bW, wcol, OUTD, bOUTD, tag):
            for j in range(4):
                for c in range(NCH):
                    P.add("tensor",
                          lambda eng, c=c, j=j, WAP=WAP, wcol=wcol, XWa=XWa: eng.matmul(
                              ps[1][:, 0:512], XWa[:, c, j * 128:(j + 1) * 128],
                              WAP[:, c, wcol:wcol + 512],
                              start=(c == 0), stop=(c == NCH - 1)),
                          reads=(bW, bXW), writes=(bps[1],), deps=(ld,), tag=tag + "mm")
                ev, bEV = al_ev[tag]
                P.add("vector",
                      lambda eng, ev=ev: eng.tensor_copy(ev, ps[1][:, 0:512]),
                      reads=(bps[1],), writes=(bEV,), tag=tag + "ev")
                kt = (w0 // 128) + j
                P.add("sync",
                      lambda eng, ev=ev, OUTD=OUTD, kt=kt: eng.dma_start(
                          out=OUTD.ap().rearrange("(k p) e -> p k e", p=128)[:, kt, :],
                          in_=ev),
                      reads=(bEV,), writes=(bOUTD,), kind="dma", tag=tag + "st")

        vproj(WQKV, bWQKV, 1024, vfd, bvfd, "vf")
        vproj(WVR, bWVR, 0, vrd, bvrd, "vr")

    P.barrier()
    al.reset(markA)

    def take2(shape, dtype, nm):
        a = al.take(shape, dtype, nm + "0")
        b = al.take(shape, dtype, nm + "1")
        return [b, a] if SWAPALLOC else [a, b]
    KFQ = take2((64, 8, 1152), BF16, "KFQ")
    KRQ = take2((32, 8, 384), BF16, "KRQ")
    VFQ = take2((128, 9, 512), BF16, "VFQ")
    VRQ = take2((128, 3, 512), BF16, "VRQ")
    QFQ = take2((64, 8, 128), BF16, "QFQ")
    QRQ = take2((32, 8, 128), BF16, "QRQ")
    XQ = take2((128, NCH, 128), F32, "XQ")
    BAND = take2((128, 1152), F32, "BAND")
    PBF = take2((128, 1152), BF16, "PBF")
    PTS = al.take((128, 128), BF16, "PTS")
    OSB = [al.take((128, 1024), BF16, f"OSB{i}") for i in range(2)]
    OTR = al.take((128, 1024), BF16, "OTR")
    RES = [al.take((128, NCH, 128), F32, f"RES{i}") for i in range(2)]
    STAT, bSTAT = al.take((128, 64), F32, "STAT")
    LNR, bLNR = al.take((128, 512), F32, "LNR")

    def bufsel(arr, par, nm):
        return arr[par if nm not in FORCE1 else 1]

    def prefetch(qt):
        par = 0 if DBG_SINGLEBUF else qt % 2
        deps = []
        kt_lo_f, kt_lo_r = qt, qt + 6
        d1 = P.add("sync", lambda eng, qt=qt, par=par: eng.dma_start(
            out=bufsel(KFQ, par, 'KFQ')[0],
            in_=kfd.ap().rearrange("(h e) t -> e h t", e=64)[:, :, kt_lo_f * 128:kt_lo_f * 128 + 1152]),
            reads=(bkfd,), writes=(bufsel(KFQ, par, 'KFQ')[1],), kind="dma", tag=f"pfKF{qt}")
        d2 = P.add("sync", lambda eng, qt=qt, par=par: eng.dma_start(
            out=bufsel(KRQ, par, 'KRQ')[0],
            in_=krd.ap().rearrange("(h e) t -> e h t", e=32)[:, :, kt_lo_r * 128:kt_lo_r * 128 + 384]),
            reads=(bkrd,), writes=(bufsel(KRQ, par, 'KRQ')[1],), kind="dma", tag=f"pfKR{qt}")
        d3 = P.add("sync", lambda eng, qt=qt, par=par: eng.dma_start(
            out=bufsel(VFQ, par, 'VFQ')[0],
            in_=vfd.ap().rearrange("(k p) e -> p k e", p=128)[:, kt_lo_f:kt_lo_f + 9, :]),
            reads=(bvfd,), writes=(bufsel(VFQ, par, 'VFQ')[1],), kind="dma", tag=f"pfVF{qt}")
        d4 = P.add("sync", lambda eng, qt=qt, par=par: eng.dma_start(
            out=bufsel(VRQ, par, 'VRQ')[0],
            in_=vrd.ap().rearrange("(k p) e -> p k e", p=128)[:, kt_lo_r:kt_lo_r + 3, :]),
            reads=(bvrd,), writes=(bufsel(VRQ, par, 'VRQ')[1],), kind="dma", tag=f"pfVR{qt}")
        d5 = P.add("sync", lambda eng, qt=qt, par=par: eng.dma_start(
            out=bufsel(QFQ, par, 'QFQ')[0],
            in_=qfd.ap().rearrange("(h e) t -> e h t", e=64)[:, :, qt * 128:(qt + 1) * 128]),
            reads=(bqfd,), writes=(bufsel(QFQ, par, 'QFQ')[1],), kind="dma", tag=f"pfQF{qt}")
        d6 = P.add("sync", lambda eng, qt=qt, par=par: eng.dma_start(
            out=bufsel(QRQ, par, 'QRQ')[0],
            in_=qrd.ap().rearrange("(h e) t -> e h t", e=32)[:, :, qt * 128:(qt + 1) * 128]),
            reads=(bqrd,), writes=(bufsel(QRQ, par, 'QRQ')[1],), kind="dma", tag=f"pfQR{qt}")
        d7 = P.add("sync", lambda eng, qt=qt, par=par: eng.dma_start(
            out=bufsel(XQ, par, 'XQ')[0],
            in_=rearr_cp(io["xT"].ap())[:, :, qt * 128:(qt + 1) * 128]),
            writes=(bufsel(XQ, par, 'XQ')[1],), kind="dma", tag=f"pfXQ{qt}")
        return [d1, d2, d3, d4, d5, d6, d7]

    pf = prefetch(0)
    for qt in range(QT):
        par = qt % 2
        if DBG_SINGLEBUF:
            par = 0
        KFa, bKF = KFQ[par if "KFQ" not in FORCE1 else 1]
        KRa, bKR = KRQ[par if "KRQ" not in FORCE1 else 1]
        VFa, bVF = VFQ[par if "VFQ" not in FORCE1 else 1]
        VRa, bVR = VRQ[par if "VRQ" not in FORCE1 else 1]
        QFa, bQF = QFQ[par if "QFQ" not in FORCE1 else 1]
        QRa, bQR = QRQ[par if "QRQ" not in FORCE1 else 1]
        XQa, bXQ = XQ[par if "XQ" not in FORCE1 else 1]
        BANDa, bBAND = BAND[0 if "BAND" in FORCE0 else (par if "BAND" not in FORCE1 else 1)]
        PBFa, bPBF = PBF[0 if "PBF" in FORCE0 else (par if "PBF" not in FORCE1 else 1)]
        PTSa, bPTS = PTS
        OSBa, bOSB = OSB[0 if "OSB" in FORCE0 else (par if "OSB" not in FORCE1 else 1)]
        OTRa, bOTR = OTR
        RESa, bRES = RES[0 if "RES" in FORCE0 else (par if "RES" not in FORCE1 else 1)]

        if qt + 1 < QT:
            nxt = prefetch(qt + 1)

        kt_lo_f = qt
        for h in range(16):
            full = h < 8
            if full:
                qsl = QFa[0:64, h, :]
                nkt = 9
                Ka, bKa, hh2, kdim = KFa, bKF, h, 64
                Va, bVa, vh = VFa, bVF, h * 64
            else:
                hh = h - 8
                qsl = QRa[0:32, hh, :]
                nkt = 3
                Ka, bKa, hh2, kdim = KRa, bKR, hh, 32
                Va, bVa, vh = VRa, bVR, hh * 64
            bw = nkt * 128

            def sc_bank(j):
                bi = j // 4
                return ps[bi][:, (j % 4) * 128:(j % 4) * 128 + 128], bps[bi]
            for j in range(nkt):
                dst, bD = sc_bank(j)
                kap = Ka[0:kdim, hh2, j * 128:(j + 1) * 128]
                P.add("tensor",
                      lambda eng, qsl=qsl, kap=kap, dst=dst: eng.matmul(
                          dst, qsl, kap, start=True, stop=True),
                      reads=(bQF if full else bQR, bKa), writes=(bD,),
                      deps=pf, tag=f"sc{qt}_{h}")
            if DBG_DUMP and qt == 0 and h == 0:
                P.add("vector",
                      lambda eng: eng.tensor_copy(OTRa.bitcast(F32)[:, 0:128], ps[0][:, 0:128]),
                      reads=(bps[0],), writes=(bOTR,), tag="dbgcp")
                P.add("scalar",
                      lambda eng: eng.dma_start(out=io["ps00"].ap(), in_=OTRa.bitcast(F32)[:, 0:128]),
                      reads=(bOTR,), writes=(io["b_ps00"],), kind="dma", tag="dbgps0")
            P.add("vector",
                  lambda eng, BANDa=BANDa: eng.tensor_add(BANDa[:, 0:128], ps[0][:, 0:128],
                                             BMASK[:, 0:128]),
                  reads=(bps[0], bBMASK), writes=(bBAND,), tag="evfar")
            if DBG_DUMP and qt == 0 and h == 0:
                P.add("scalar",
                      lambda eng: eng.dma_start(out=io["bandfar"].ap(), in_=BANDa[:, 0:128]),
                      reads=(bBAND,), writes=(io["b_bandfar"],), kind="dma", tag="dbgbf")
            if nkt > 2:
                hi = min(nkt - 1, 4)
                P.add("vector",
                      lambda eng, hi=hi, BANDa=BANDa: eng.tensor_copy(BANDa[:, 128:hi * 128],
                                                         ps[0][:, 128:hi * 128]),
                      reads=(bps[0],), writes=(bBAND,), tag="evmidA")
                if nkt - 1 > 4:
                    P.add("vector",
                          lambda eng, nkt=nkt, BANDa=BANDa: eng.tensor_copy(
                              BANDa[:, 512:(nkt - 1) * 128],
                              ps[1][:, 0:(nkt - 5) * 128]),
                          reads=(bps[1],), writes=(bBAND,), tag="evmidB")
            lj = nkt - 1
            ldst, lbD = sc_bank(lj)
            P.add("vector",
                  lambda eng, ldst=ldst, lj=lj, BANDa=BANDa: eng.tensor_add(
                      BANDa[:, lj * 128:lj * 128 + 128], ldst, BMASK[:, 1024:1152]),
                  reads=(lbD, bBMASK), writes=(bBAND,), tag="evcau")
            kt_lo = kt_lo_f if full else qt + 6
            nhv = 8 - kt_lo
            if nhv > 0:
                w = min(nhv, nkt) * 128
                P.add("vector",
                      lambda eng, kt_lo=kt_lo, w=w, BANDa=BANDa: eng.tensor_add(
                          BANDa[:, 0:w], BANDa[:, 0:w],
                          VALID[:, kt_lo * 128:kt_lo * 128 + w]),
                      reads=(bVALID,), writes=(bBAND,), tag="evhv")
            if DBG_DUMP and qt in (0, 2) and h == 0:
                nmm = "bmask0" if qt == 0 else "bmask2"
                P.add("scalar",
                      lambda eng, nmm=nmm: eng.dma_start(out=io[nmm].ap(), in_=BMASK),
                      reads=(bBMASK,), writes=(io["b_" + nmm],), kind="dma", tag="dbgbm")
                if qt == 0:
                    P.add("scalar",
                          lambda eng: eng.dma_start(out=io["valid0"].ap(), in_=VALID),
                          reads=(bVALID,), writes=(io["b_valid0"],), kind="dma", tag="dbgvld")
            if DBG_DUMP and qt in (0, 1) and h == 0:
                nm = "band0" if qt == 0 else "band1"
                P.add("scalar",
                      lambda eng, nm=nm, BANDa=BANDa: eng.dma_start(
                          out=io[nm].ap(), in_=BANDa),
                      reads=(bBAND,), writes=(io["b_" + nm],), kind="dma", tag="dbgband")
            if DBG_DUMP and qt == 0 and h == 0:
                P.add("scalar",
                      lambda eng: eng.dma_start(out=io["bandhv"].ap(), in_=BANDa[:, 0:256]),
                      reads=(bBAND,), writes=(io["b_bandhv"],), kind="dma", tag="dbghv")
            P.add("vector",
                  lambda eng, h=h, bw=bw, BANDa=BANDa: eng.reduce_max(
                      STAT[:, h:h + 1], BANDa[:, 0:bw], axis=AX.X, negate=True),
                  reads=(bBAND,), writes=(bSTAT,), tag="negmax")
            P.add("scalar",
                  lambda eng, h=h, bw=bw, PBFa=PBFa, BANDa=BANDa: eng.activation(
                      PBFa[:, 0:bw], BANDa[:, 0:bw], ACTF.Exp,
                      bias=STAT[:, h:h + 1], accum_out=STAT[:, 32 + h:33 + h]),
                  reads=(bBAND, bSTAT), writes=(bPBF, bSTAT), tag="exp")
            if DBG_DUMP and qt == 0 and h == 0:
                P.add("scalar",
                      lambda eng, PBFa=PBFa: eng.dma_start(
                          out=io["pbf0"].ap(), in_=PBFa),
                      reads=(bPBF,), writes=(io["b_pbf0"],), kind="dma", tag="dbgpbf")
            for j in range(nkt):
                P.add("tensor",
                      lambda eng, j=j, PBFa=PBFa: eng.transpose(pb[0][:, 0:128],
                                                     PBFa[:, j * 128:(j + 1) * 128],
                                                     IDENT),
                      reads=(bPBF, bIDENT), writes=(bpb[0],), tag="ptr")
                P.add("vector",
                      lambda eng: eng.tensor_copy(PTSa, pb[0][:, 0:128]),
                      reads=(bpb[0],), writes=(bPTS,), tag="ptc")
                P.add("tensor",
                      lambda eng, j=j, vh=vh, Va=Va, nkt=nkt: eng.matmul(
                          ps[3][:, 0:64], PTSa, Va[:, j, vh:vh + 64],
                          start=(j == 0), stop=(j == nkt - 1)),
                      reads=(bPTS, bVa), writes=(bps[3],), tag="pv")
            def em_recip(eng, h=h):
                eng.reciprocal(STAT[:, 48 + h:49 + h], STAT[:, 32 + h:33 + h])
                return eng.drain()
            P.add("vector", em_recip, reads=(bSTAT,), writes=(bSTAT,), tag="recip")
            P.add("vector",
                  lambda eng, h=h, OSBa=OSBa: eng.tensor_scalar(
                      OSBa[:, h * 64:(h + 1) * 64], ps[3][:, 0:64],
                      STAT[:, 48 + h:49 + h], None, ALU.mult),
                  reads=(bps[3], bSTAT), writes=(bOSB,), tag="oev")

        for c in range(NCH):
            P.add("tensor",
                  lambda eng, c=c, OSBa=OSBa: eng.transpose(pb[1][:, 0:128],
                                                 OSBa[:, c * 128:(c + 1) * 128], IDENT),
                  reads=(bOSB, bIDENT), writes=(bpb[1],), tag="otr")
            P.add("vector",
                  lambda eng, c=c: eng.tensor_copy(OTRa[:, c * 128:(c + 1) * 128],
                                                   pb[1][:, 0:128]),
                  reads=(bpb[1],), writes=(bOTR,), tag="otc")
        for ec in range(NCH):
            bk = 4 + (ec % 2)
            for fc in range(NCH):
                P.add("tensor",
                      lambda eng, ec=ec, fc=fc, bk=bk: eng.matmul(
                          ps[bk][:, 0:128], WPRJ[:, fc, ec * 128:(ec + 1) * 128],
                          OTRa[:, fc * 128:(fc + 1) * 128],
                          start=(fc == 0), stop=(fc == NCH - 1)),
                      reads=(bWPRJ, bOTR), writes=(bps[bk],), tag="yprj")
            P.add("vector",
                  lambda eng, ec=ec, bk=bk, RESa=RESa, XQa=XQa: eng.tensor_add(
                      RESa[:, ec, :], ps[bk][:, 0:128], XQa[:, ec, :]),
                  reads=(bps[bk], bXQ), writes=(bRES,), deps=pf, tag="resadd")
        if DBG_DUMP and qt in (0, 1):
            nm = "kfq0" if qt == 0 else "kfq1"
            P.add("scalar",
                  lambda eng, KFa=KFa, nm=nm: eng.dma_start(
                      out=io[nm].ap().rearrange("(h e) t -> e h t", e=64), in_=KFa),
                  reads=(bKF,), writes=(io["b_" + nm],), kind="dma", tag="dbgkf")
            if qt == 0:
                P.add("scalar",
                      lambda eng, VFa=VFa: eng.dma_start(
                          out=io["vfq0"].ap().rearrange("(k p) e -> p k e", p=128), in_=VFa),
                      reads=(bVF,), writes=(io["b_vfq0"],), kind="dma", tag="dbgvf")
                P.add("scalar",
                      lambda eng, QFa=QFa: eng.dma_start(
                          out=io["qfq0"].ap().rearrange("(h e) t -> e h t", e=64), in_=QFa),
                      reads=(bQF,), writes=(io["b_qfq0"],), kind="dma", tag="dbgqf")
        if DBG_DUMP:
            P.add("scalar",
                  lambda eng, qt=qt, RESa=RESa: eng.dma_start(
                      out=rearr_cp(io["resd"].ap())[:, :, qt * 128:(qt + 1) * 128],
                      in_=RESa),
                  reads=(bRES,), writes=(io["b_resd"],), kind="dma", tag=f"dbgres{qt}")
            P.add("scalar",
                  lambda eng, qt=qt, OSBa=OSBa: eng.dma_start(
                      out=io["osbd"].ap().rearrange("(q p) e -> p q e", p=128)[:, qt, :],
                      in_=OSBa),
                  reads=(bOSB,), writes=(io["b_osbd"],), kind="dma", tag=f"dbgosb{qt}")
        for ec in range(NCH):
            P.add("tensor",
                  lambda eng, ec=ec, RESa=RESa: eng.matmul(ps[4][0:1, 0:128], ONES1,
                                                RESa[:, ec, :],
                                                start=(ec == 0), stop=(ec == NCH - 1)),
                  reads=(bONES1, bRES), writes=(bps[4],), tag="lnsum")
        for ec in range(NCH):
            P.add("scalar",
                  lambda eng, ec=ec, BANDa=BANDa, RESa=RESa: eng.activation(
                      BANDa[:, ec * 128:(ec + 1) * 128], RESa[:, ec, :], ACTF.Square),
                  reads=(bRES,), writes=(bBAND,), tag="lnsq")
            P.add("tensor",
                  lambda eng, ec=ec, BANDa=BANDa: eng.matmul(ps[5][0:1, 0:128], ONES1,
                                                BANDa[:, ec * 128:(ec + 1) * 128],
                                                start=(ec == 0), stop=(ec == NCH - 1)),
                  reads=(bONES1, bBAND), writes=(bps[5],), tag="lnsumsq")
        P.add("vector",
              lambda eng: eng.tensor_scalar_mul(LNR[0:1, 0:128], ps[4][0:1, 0:128],
                                                1.0 / 1024.0),
              reads=(bps[4],), writes=(bLNR,), tag="lnmean")
        P.add("vector",
              lambda eng: eng.tensor_scalar_mul(LNR[0:1, 128:256], ps[5][0:1, 0:128],
                                                1.0 / 1024.0),
              reads=(bps[5],), writes=(bLNR,), tag="lnms")
        P.add("vector",
              lambda eng: eng.tensor_mul(LNR[0:1, 256:384], LNR[0:1, 0:128],
                                         LNR[0:1, 0:128]),
              writes=(bLNR,), tag="lnm2")
        P.add("vector",
              lambda eng: eng.scalar_tensor_tensor(LNR[0:1, 128:256], LNR[0:1, 128:256],
                                                   1e-5, LNR[0:1, 256:384],
                                                   ALU.add, ALU.subtract),
              writes=(bLNR,), tag="lnvar")
        P.add("scalar",
              lambda eng: eng.activation(LNR[0:1, 256:384], LNR[0:1, 128:256],
                                         ACTF.Sqrt),
              writes=(bLNR,), tag="lnsqrt")
        def em_lnrecip(eng):
            eng.reciprocal(LNR[0:1, 128:256], LNR[0:1, 256:384])
            return eng.drain()
        P.add("vector", em_lnrecip, writes=(bLNR,), tag="lnrecip")
        P.add("tensor",
              lambda eng: eng.matmul(ps[4][:, 0:128], ONESR[0:1, :],
                                     LNR[0:1, 0:128], start=True, stop=True),
              reads=(bONESR, bLNR), writes=(bps[4],), tag="lnbm")
        P.add("tensor",
              lambda eng: eng.matmul(ps[5][:, 0:128], ONESR[0:1, :],
                                     LNR[0:1, 128:256], start=True, stop=True),
              reads=(bONESR, bLNR), writes=(bps[5],), tag="lnbr")
        P.add("vector",
              lambda eng: eng.tensor_copy(LNR[:, 256:384], ps[4][:, 0:128]),
              reads=(bps[4],), writes=(bLNR,), tag="lnbmc")
        P.add("vector",
              lambda eng: eng.tensor_copy(LNR[:, 384:512], ps[5][:, 0:128]),
              reads=(bps[5],), writes=(bLNR,), tag="lnbrc")
        for ec in range(NCH):
            P.add("vector",
                  lambda eng, ec=ec, RESa=RESa: eng.tensor_sub(RESa[:, ec, :], RESa[:, ec, :],
                                                    LNR[:, 256:384]),
                  reads=(bLNR,), writes=(bRES,), tag="lnsub")
            P.add("vector",
                  lambda eng, ec=ec, RESa=RESa: eng.tensor_mul(RESa[:, ec, :], RESa[:, ec, :],
                                                    LNR[:, 384:512]),
                  reads=(bLNR,), writes=(bRES,), tag="lnmul")
        P.add("sync",
              lambda eng, qt=qt, RESa=RESa: eng.dma_start(
                  out=rearr_cp(io["x1d"].ap())[:, :, qt * 128:(qt + 1) * 128],
                  in_=RESa),
              reads=(bRES,), writes=(io["b_x1d"],), kind="dma", tag=f"stx1_{qt}")
        if qt + 1 < QT:
            pf = nxt
        if DBG_BARRIER:
            P.barrier()

    P.barrier()
    al.reset(mark0)





def bcast_ap(ap2d, axes):
    step = ap2d.ap[1][0]
    n = ap2d.ap[1][1]
    if axes == "a":
        dims = [ap2d.ap[0], [step, n], [0, n]]
    else:
        dims = [ap2d.ap[0], [0, n], [step, n]]
    return bass.AP(ap2d.tensor, ap2d.offset, dims)


def build_ln_helper(P, al, ps, bps, ONES1, bONES1, ONESR, bONESR):
    LNR, bLNR = al.take((128, 512), F32, "LNRg")
    LNSQ, bLNSQ = al.take((128, 1024), F32, "LNSQg")

    def ln_store(P, RESap, bRES, io_dst, bdst, qt, tag):
        for ec in range(NCH):
            P.add("tensor",
                  lambda eng, ec=ec, RESap=RESap: eng.matmul(
                      ps[4][0:1, 0:128], ONES1, RESap[:, ec, :],
                      start=(ec == 0), stop=(ec == NCH - 1)),
                  reads=(bONES1, bRES), writes=(bps[4],), tag=tag + "lnsum")
        for ec in range(NCH):
            P.add("scalar",
                  lambda eng, ec=ec, RESap=RESap: eng.activation(
                      LNSQ[:, ec * 128:(ec + 1) * 128], RESap[:, ec, :], ACTF.Square),
                  reads=(bRES,), writes=(bLNSQ,), tag=tag + "lnsq")
            P.add("tensor",
                  lambda eng, ec=ec: eng.matmul(
                      ps[5][0:1, 0:128], ONES1, LNSQ[:, ec * 128:(ec + 1) * 128],
                      start=(ec == 0), stop=(ec == NCH - 1)),
                  reads=(bONES1, bLNSQ), writes=(bps[5],), tag=tag + "lnsumsq")
        P.add("vector", lambda eng: eng.tensor_scalar_mul(
            LNR[0:1, 0:128], ps[4][0:1, 0:128], 1.0 / 1024.0),
            reads=(bps[4],), writes=(bLNR,), tag=tag + "lnmean")
        P.add("vector", lambda eng: eng.tensor_scalar_mul(
            LNR[0:1, 128:256], ps[5][0:1, 0:128], 1.0 / 1024.0),
            reads=(bps[5],), writes=(bLNR,), tag=tag + "lnms")
        P.add("vector", lambda eng: eng.tensor_mul(
            LNR[0:1, 256:384], LNR[0:1, 0:128], LNR[0:1, 0:128]),
            writes=(bLNR,), tag=tag + "lnm2")
        P.add("vector", lambda eng: eng.scalar_tensor_tensor(
            LNR[0:1, 128:256], LNR[0:1, 128:256], 1e-5, LNR[0:1, 256:384],
            ALU.add, ALU.subtract), writes=(bLNR,), tag=tag + "lnvar")
        P.add("scalar", lambda eng: eng.activation(
            LNR[0:1, 256:384], LNR[0:1, 128:256], ACTF.Sqrt),
            writes=(bLNR,), tag=tag + "lnsqrt")
        def em_rec(eng):
            eng.reciprocal(LNR[0:1, 128:256], LNR[0:1, 256:384])
            return eng.drain()
        P.add("vector", em_rec, writes=(bLNR,), tag=tag + "lnrecip")
        P.add("tensor", lambda eng: eng.matmul(
            ps[4][:, 0:128], ONESR[0:1, :], LNR[0:1, 0:128], start=True, stop=True),
            reads=(bONESR, bLNR), writes=(bps[4],), tag=tag + "lnbm")
        P.add("tensor", lambda eng: eng.matmul(
            ps[5][:, 0:128], ONESR[0:1, :], LNR[0:1, 128:256], start=True, stop=True),
            reads=(bONESR, bLNR), writes=(bps[5],), tag=tag + "lnbr")
        P.add("vector", lambda eng: eng.tensor_copy(LNR[:, 256:384], ps[4][:, 0:128]),
              reads=(bps[4],), writes=(bLNR,), tag=tag + "lnbmc")
        P.add("vector", lambda eng: eng.tensor_copy(LNR[:, 384:512], ps[5][:, 0:128]),
              reads=(bps[5],), writes=(bLNR,), tag=tag + "lnbrc")
        for ec in range(NCH):
            P.add("vector", lambda eng, ec=ec, RESap=RESap: eng.tensor_sub(
                RESap[:, ec, :], RESap[:, ec, :], LNR[:, 256:384]),
                reads=(bLNR,), writes=(bRES,), tag=tag + "lnsub")
            P.add("vector", lambda eng, ec=ec, RESap=RESap: eng.tensor_mul(
                RESap[:, ec, :], RESap[:, ec, :], LNR[:, 384:512]),
                reads=(bLNR,), writes=(bRES,), tag=tag + "lnmul")
        P.add("sync", lambda eng, qt=qt, RESap=RESap: eng.dma_start(
            out=rearr_cp(io_dst.ap())[:, :, qt * 128:(qt + 1) * 128], in_=RESap),
            reads=(bRES,), writes=(bdst,), kind="dma", tag=tag + f"st{qt}")

    return ln_store


def build_stage2(P, al, io, ps, bps, pb, bpb):
    nc = P.nc
    QT = TC // 128
    mark0 = al.mark()

    X1B, bX1B = al.take((128, NCH, TC), BF16, "X1B")
    QB, bQB = al.take((128, 16, TC), BF16, "QB")
    KEYST, bKEYST = al.take((128, 16, 256), BF16, "KEYST")
    QS, bQS = al.take((128, 64), F32, "QS")
    GAM, bGAM = al.take((128, 16), F32, "GAM")
    BET, bBET = al.take((128, 16), F32, "BET")
    IOTA8, bIOTA8 = al.take((128, 8), F32, "IOTA8")
    ONES1, bONES1 = al.take((128, 1), F32, "ONES1s2")
    ONESR, bONESR = al.take((128, 128), F32, "ONESRs2")
    IDENTB, bIDENTB = al.take((128, 128), BF16, "IDENTBs2")
    IDENTF, bIDENTF = al.take((128, 128), F32, "IDENTFs2")

    x1d, bx1d = io["x1d"], io["b_x1d"]
    x2d, bx2d = io["x2d"], io["b_x2d"]

    P.dma("gpsimd", X1B, rearr_cp(x1d.ap()), reads=(bx1d,), writes=(bX1B,))
    P.dma("gpsimd", KEYST, io["keysT"].ap().rearrange("(g d) k -> d g k", d=128),
          writes=(bKEYST,))
    P.dma("sync", GAM, io["bn_gamma"].ap(), writes=(bGAM,))
    P.dma("sync", BET, io["bn_beta"].ap(), writes=(bBET,))
    P.dma("sync", IOTA8, io["iota8"].ap(), writes=(bIOTA8,))
    P.dma("sync", IDENTB, io["identb"].ap(), writes=(bIDENTB,))
    P.dma("sync", IDENTF, io["identf"].ap(), writes=(bIDENTF,))
    P.add("vector", lambda eng: eng.memset(ONES1, 1.0), writes=(bONES1,))
    P.add("vector", lambda eng: eng.memset(ONESR[0:1, :], 1.0), writes=(bONESR,))
    P.add("vector", lambda eng: eng.memset(QS[:, 0:32], 0.0), writes=(bQS,))

    ln_store = build_ln_helper(P, al, ps, bps, ONES1, bONES1, ONESR, bONESR)

    markA = al.mark()
    WQ, bWQ = al.take((128, NCH, 2048), BF16, "WQ")
    SQ, bSQ = al.take((128, 512), F32, "SQs2")
    P.dma("gpsimd", WQ, rearr_cp(io["wq_pk"].ap()), writes=(bWQ,))

    for t4 in range(TC // 512 if _os00.environ.get("BASSS2Q", "1") == "1" else 0):
        for e in range(16):
            for c in range(NCH):
                P.add("tensor",
                      lambda eng, t4=t4, e=e, c=c: eng.matmul(
                          ps[0][:, 0:512], WQ[:, c, e * 128:(e + 1) * 128],
                          X1B[:, c, t4 * 512:(t4 + 1) * 512],
                          start=(c == 0), stop=(c == NCH - 1)),
                      reads=(bWQ, bX1B), writes=(bps[0],), tag="qmm")
            P.add("scalar",
                  lambda eng, t4=t4, e=e: eng.activation(
                      QB[:, e, t4 * 512:(t4 + 1) * 512], ps[0][:, 0:512], ACTF.Copy,
                      accum_out=QS[:, 62:63]),
                  reads=(bps[0],), writes=(bQB, bQS), tag="qev")
            P.add("vector",
                  lambda eng, e=e: eng.tensor_add(QS[:, e:e + 1], QS[:, e:e + 1],
                                                  QS[:, 62:63]),
                  writes=(bQS,), tag="qsacc")
            P.add("scalar",
                  lambda eng, t4=t4, e=e: eng.activation(
                      SQ, QB[:, e, t4 * 512:(t4 + 1) * 512], ACTF.Square,
                      accum_out=QS[:, 63:64]),
                  reads=(bQB,), writes=(bSQ, bQS), tag="qsq")
            P.add("vector",
                  lambda eng, e=e: eng.tensor_add(QS[:, 16 + e:17 + e],
                                                  QS[:, 16 + e:17 + e], QS[:, 63:64]),
                  writes=(bQS,), tag="qsqacc")

    bnin, b_bnin = io["bnin"], Buf("bnin")
    bnout, b_bnout = io["bnout"], Buf("bnout")
    if DBG_DUMP:
        P.add("scalar", lambda eng: eng.dma_start(out=io["d_qs0"].ap(), in_=QS[:, 0:32]),
              reads=(bQS,), writes=(io["b_d_qs0"],), kind="dma", tag="dbgqs0")
        P.add("scalar", lambda eng: eng.dma_start(
            out=io["d_qbr"].ap().rearrange("p (g t) -> p g t", g=16)[:, :, 0:128],
            in_=QB[:, :, 0:128]),
            reads=(bQB,), writes=(io["b_d_qbr"],), kind="dma", tag="dbgqbr")
    P.add("sync", lambda eng: eng.dma_start(out=bnin.ap(), in_=QS[:, 0:32]),
          reads=(bQS,), writes=(b_bnin,), kind="dma", tag="bnst")
    if _os00.environ.get("BASSNOCC", "") == "1":
        P.dma("sync", bnout.ap(), bnin.ap(), reads=(b_bnin,), writes=(b_bnout,), tag="bnccfake")
    else:
        P.add("gpsimd", lambda eng: eng.collective_compute(
            "AllReduce", mybir.AluOpType.add, replica_groups=[list(range(8))],
            ins=[bnin.ap()], outs=[bnout.ap()]),
            reads=(b_bnin,), writes=(b_bnout,), kind="cc", tag="bncc")
    P.add("sync", lambda eng: eng.dma_start(out=QS[:, 0:32], in_=bnout.ap()),
          reads=(b_bnout,), writes=(bQS,), kind="dma", tag="bnld")
    if DBG_DUMP:
        P.add("scalar", lambda eng: eng.dma_start(out=io["d_qs1"].ap(), in_=QS[:, 0:32]),
              reads=(bQS,), writes=(io["b_d_qs1"],), kind="dma", tag="dbgqs1")
    P.add("vector", lambda eng: eng.tensor_scalar_mul(QS[:, 32:48], QS[:, 0:16],
                                                      1.0 / 16384.0),
          writes=(bQS,), tag="bnmean")
    P.add("vector", lambda eng: eng.tensor_scalar_mul(QS[:, 16:32], QS[:, 16:32],
                                                      1.0 / 16384.0),
          writes=(bQS,), tag="bnms")
    P.add("vector", lambda eng: eng.tensor_mul(QS[:, 48:64], QS[:, 32:48], QS[:, 32:48]),
          writes=(bQS,), tag="bnm2")
    P.add("vector", lambda eng: eng.scalar_tensor_tensor(
        QS[:, 16:32], QS[:, 16:32], 1e-5, QS[:, 48:64], ALU.add, ALU.subtract),
        writes=(bQS,), tag="bnvar")
    if DBG_DUMP:
        P.add("scalar", lambda eng: eng.dma_start(out=io["d_var"].ap(), in_=QS[:, 16:32]),
              reads=(bQS,), writes=(io["b_d_var"],), kind="dma", tag="dbgvar")
    P.add("scalar", lambda eng: eng.activation(QS[:, 48:64], QS[:, 16:32], ACTF.Sqrt),
          writes=(bQS,), tag="bnsqrt")
    def em_bnrec(eng):
        eng.reciprocal(QS[:, 16:32], QS[:, 48:64])
        return eng.drain()
    P.add("vector", em_bnrec, writes=(bQS,), tag="bnrecip")
    P.add("vector", lambda eng: eng.tensor_mul(QS[:, 48:64], QS[:, 16:32], GAM),
          reads=(bGAM,), writes=(bQS,), tag="bnscale")
    P.add("vector", lambda eng: eng.tensor_mul(QS[:, 32:48], QS[:, 32:48], QS[:, 48:64]),
          writes=(bQS,), tag="bnmsc")
    P.add("vector", lambda eng: eng.tensor_sub(QS[:, 32:48], BET, QS[:, 32:48]),
          reads=(bBET,), writes=(bQS,), tag="bnshift")
    for e in range(16 if _os00.environ.get("BASSS2A", "1") == "1" else 0):
        P.add("scalar",
              lambda eng, e=e: eng.activation(QB[:, e, :], QB[:, e, :], ACTF.Identity,
                                              bias=QS[:, 32 + e:33 + e],
                                              scale=QS[:, 48 + e:49 + e]),
              reads=(bQS,), writes=(bQB,), tag="bnapply")

    if DBG_DUMP:
        P.add("scalar", lambda eng: eng.dma_start(out=io["d_qs"].ap(), in_=QS),
              reads=(bQS,), writes=(io["b_d_qs"],), kind="dma", tag="dbgqs")
        P.add("scalar", lambda eng: eng.dma_start(
            out=io["d_qb"].ap().rearrange("p (g t) -> p g t", g=16)[:, :, 0:128],
            in_=QB[:, :, 0:128]),
            reads=(bQB,), writes=(io["b_d_qb"],), kind="dma", tag="dbgqb")
    P.barrier()
    al.reset(markA)

    X1Q = [al.take((128, NCH, 128), F32, f"S2XQ{i}") for i in range(2)]
    X1TOK, bX1TOK = al.take((128, 1024), BF16, "X1TOK")
    SXF, bSXF = al.take((128, 16, 8), F32, "SXF")
    IXF, bIXF = al.take((128, 16, 8), F32, "IXF")
    MX8, bMX8 = al.take((128, 8), F32, "MX8")
    IX8, bIX8 = al.take((128, 8), U32, "IX8")
    ASC, bASC = al.take((128, 64), F32, "ASC")
    S8, bS8 = al.take((128, 8, 8), F32, "S8")
    PK8, bPK8 = al.take((128, 8), U32, "PK8")
    PKF, bPKF = al.take((128, 16), F32, "PKF")
    EQ, bEQ = al.take((128, 64), F32, "EQ")
    SEL, bSEL = al.take((128, 16), F32, "SEL")
    IDX8F, bIDX8F = al.take((128, 8, 8), F32, "IDX8F")
    IDX8I, bIDX8I = al.take((128, 8, 8), I32, "IDX8I")
    HD, bHD = al.take((128, 8, 8), F32, "HD")
    W8, bW8 = al.take((128, 8, 8), F32, "W8")
    SS, bSS = al.take((128, 24), F32, "SS")
    GD = [al.take((128, 1024), BF16, f"GD{i}") for i in range(4)]
    GU = [al.take((128, 1024), BF16, f"GU{i}") for i in range(4)]
    PRD, bPRD = al.take((128, 1024), F32, "PRD")
    WTT, bWTT = al.take((64, 128), F32, "WTT")
    IDXT, bIDXT = al.take((64, 128), F32, "IDXT")
    WB2, bWB2 = al.take((128, 128), BF16, "WB2")
    TIDX2, bTIDX2 = al.take((128, 64), I32, "TIDX2")
    OUTF, bOUTF = al.take((128, NCH, 128), F32, "OUTF")

    dtab, utab = io["down_embed"], io["up_embed"]

    def ldx1(qt, par):
        XQa, bXQ = X1Q[par]
        return P.add("sync", lambda eng, qt=qt, XQa=XQa: eng.dma_start(
            out=XQa, in_=rearr_cp(x1d.ap())[:, :, qt * 128:(qt + 1) * 128]),
            reads=(bx1d,), writes=(bXQ,), kind="dma", tag=f"s2ldx{qt}")

    pfx = ldx1(0, 0)
    for qt in range(QT if S2LVL >= 2 else 0):
        par = qt % 2
        XQa, bXQ = X1Q[par]
        if qt + 1 < QT:
            nxt = ldx1(qt + 1, (qt + 1) % 2)
        tsl = slice(qt * 128, (qt + 1) * 128)
        for c in range(NCH):
            P.add("tensor", lambda eng, c=c, tsl=tsl: eng.transpose(
                pb[0][:, 0:128], X1B[:, c, tsl], IDENTB),
                reads=(bX1B, bIDENTB), writes=(bpb[0],), tag="s2xtr")
            P.add("vector", lambda eng, c=c: eng.tensor_copy(
                X1TOK[:, c * 128:(c + 1) * 128], pb[0][:, 0:128]),
                reads=(bpb[0],), writes=(bX1TOK,), tag="s2xtc")
        for g in range(16):
            bk = g % 2
            P.add("tensor", lambda eng, g=g, tsl=tsl, bk=bk: eng.matmul(
                ps[bk][:, 0:256], QB[:, g, tsl], KEYST[:, g, :], start=True, stop=True),
                reads=(bQB, bKEYST), writes=(bps[bk],), tag="simmm")
            def em_max(eng, bk=bk):
                eng.max(MX8, ps[bk][:, 0:256])
                return eng.drain()
            P.add("vector", em_max, reads=(bps[bk],), writes=(bMX8,), tag="simmax")
            def em_mi(eng, bk=bk):
                eng.max_index(IX8, MX8, ps[bk][:, 0:256])
                return eng.drain()
            P.add("vector", em_mi, reads=(bMX8, bps[bk]), writes=(bIX8,), tag="simmi")
            P.add("vector", lambda eng, g=g: eng.tensor_copy(SXF[:, g, :], MX8),
                  reads=(bMX8,), writes=(bSXF,), tag="sxcp")
            P.add("vector", lambda eng, g=g: eng.tensor_copy(IXF[:, g, :], IX8),
                  reads=(bIX8,), writes=(bIXF,), tag="ixcp")
        for h in range(8 if S2LVL >= 3 else 0):
            sx, sy = SXF[:, h, :], SXF[:, 8 + h, :]
            ix, iy = IXF[:, h, :], IXF[:, 8 + h, :]
            P.add("vector", lambda eng, sx=sx, sy=sy: eng.tensor_tensor(
                ASC.rearrange("p (a b) -> p a b", a=8),
                bcast_ap(sx, "a"), bcast_ap(sy, "b"), ALU.add),
                reads=(bSXF,), writes=(bASC,), tag="comb")
            def em_max2(eng):
                eng.max(MX8, ASC)
                return eng.drain()
            P.add("vector", em_max2, reads=(bASC,), writes=(bMX8,), tag="max64")
            def em_mi2(eng):
                eng.max_index(PK8, MX8, ASC)
                return eng.drain()
            P.add("vector", em_mi2, reads=(bMX8, bASC), writes=(bPK8,), tag="mi64")
            P.add("vector", lambda eng, h=h: eng.tensor_copy(S8[:, h, :], MX8),
                  reads=(bMX8,), writes=(bS8,), tag="s8cp")
            P.add("vector", lambda eng: eng.tensor_single_scalar(
                IX8, PK8, 3.0, ALU.logical_shift_right),
                reads=(bPK8,), writes=(bIX8,), tag="pkhi")
            P.add("vector", lambda eng: eng.tensor_copy(PKF[:, 0:8], IX8),
                  reads=(bIX8,), writes=(bPKF,), tag="pkhf")
            P.add("vector", lambda eng: eng.tensor_single_scalar(
                IX8, PK8, 7.0, ALU.bitwise_and),
                reads=(bPK8,), writes=(bIX8,), tag="pklo")
            P.add("vector", lambda eng: eng.tensor_copy(PKF[:, 8:16], IX8),
                  reads=(bIX8,), writes=(bPKF,), tag="pklf")
            P.add("vector", lambda eng: eng.tensor_tensor(
                EQ.rearrange("p (a b) -> p a b", a=8),
                bcast_ap(PKF[:, 0:8], "a"), bcast_ap(IOTA8, "b"), ALU.is_equal),
                reads=(bPKF, bIOTA8), writes=(bEQ,), tag="eqhi")
            P.add("vector", lambda eng, ix=ix: eng.tensor_tensor(
                EQ.rearrange("p (a b) -> p a b", a=8),
                EQ.rearrange("p (a b) -> p a b", a=8), bcast_ap(ix, "b"), ALU.mult),
                reads=(bIXF,), writes=(bEQ,), tag="mulhi")
            P.add("vector", lambda eng: eng.tensor_reduce(
                SEL[:, 0:8], EQ.rearrange("p (a b) -> p a b", a=8),
                axis=AX.X, op=ALU.add),
                reads=(bEQ,), writes=(bSEL,), tag="redhi")
            P.add("vector", lambda eng: eng.tensor_tensor(
                EQ.rearrange("p (a b) -> p a b", a=8),
                bcast_ap(PKF[:, 8:16], "a"), bcast_ap(IOTA8, "b"), ALU.is_equal),
                reads=(bPKF, bIOTA8), writes=(bEQ,), tag="eqlo")
            P.add("vector", lambda eng, iy=iy: eng.tensor_tensor(
                EQ.rearrange("p (a b) -> p a b", a=8),
                EQ.rearrange("p (a b) -> p a b", a=8), bcast_ap(iy, "b"), ALU.mult),
                reads=(bIXF,), writes=(bEQ,), tag="mullo")
            P.add("vector", lambda eng: eng.tensor_reduce(
                SEL[:, 8:16], EQ.rearrange("p (a b) -> p a b", a=8),
                axis=AX.X, op=ALU.add),
                reads=(bEQ,), writes=(bSEL,), tag="redlo")
            P.add("vector", lambda eng, h=h: eng.scalar_tensor_tensor(
                IDX8F[:, h, :], SEL[:, 0:8], 256.0, SEL[:, 8:16], ALU.mult, ALU.add),
                reads=(bSEL,), writes=(bIDX8F,), tag="idxmk")
        P.add("vector", lambda eng: eng.tensor_copy(
            IDX8I.rearrange("p a b -> p (a b)"), IDX8F.rearrange("p a b -> p (a b)")),
            reads=(bIDX8F,), writes=(bIDX8I,), tag="idxcvt")
        if S2LVL < 3:
            P.add("vector", lambda eng: eng.memset(IDX8F.rearrange("p a b -> p (a b)"), 3.0),
                  writes=(bIDX8F,), tag="idxfake")
            P.add("vector", lambda eng: eng.memset(S8.rearrange("p a b -> p (a b)"), 0.5),
                  writes=(bS8,), tag="s8fake")
        for j in range(64 if S2LVL >= 4 else 0):
            h, k = j // 8, j % 8
            sl = GD[j % 4]
            P.add("gpsimd", lambda eng, h=h, k=k, sl=sl: eng.indirect_dma_start(
                sl[0], None, dtab.ap(),
                bass.IndirectOffsetOnAxis(IDX8I[:, h, k:k + 1], 0)),
                reads=(bIDX8I,), writes=(sl[1],), kind="dma", tag=f"gd{qt}_{j}")
            P.add("vector", lambda eng, sl=sl: eng.tensor_mul(PRD, sl[0], X1TOK),
                  reads=(sl[1], bX1TOK), writes=(bPRD,), tag="ddot")
            P.add("vector", lambda eng, h=h, k=k: eng.tensor_reduce(
                HD[:, h, k:k + 1], PRD, axis=AX.X, op=ALU.add),
                reads=(bPRD,), writes=(bHD,), tag="ddred")
        if S2LVL < 4:
            P.add("vector", lambda eng: eng.memset(HD.rearrange("p a b -> p (a b)"), 0.1),
                  writes=(bHD,), tag="hdfake")
        P.add("scalar", lambda eng: eng.activation(
            HD.rearrange("p a b -> p (a b)"), HD.rearrange("p a b -> p (a b)"),
            ACTF.Gelu_apprx_tanh), writes=(bHD,), tag="gelu")
        P.add("vector", lambda eng: eng.tensor_reduce(
            SS[:, 0:8], S8, axis=AX.X, op=ALU.max, negate=True),
            reads=(bS8,), writes=(bSS,), tag="smax")
        P.add("vector", lambda eng: eng.tensor_tensor(
            W8, S8, bcast_ap(SS[:, 0:8], "a"), ALU.add),
            reads=(bS8, bSS), writes=(bW8,), tag="ssub")
        P.add("scalar", lambda eng: eng.activation(
            W8.rearrange("p a b -> p (a b)"), W8.rearrange("p a b -> p (a b)"), ACTF.Exp),
            writes=(bW8,), tag="sexp")
        P.add("vector", lambda eng: eng.tensor_reduce(
            SS[:, 8:16], W8, axis=AX.X, op=ALU.add),
            reads=(bW8,), writes=(bSS,), tag="ssum")
        def em_rec2(eng):
            eng.reciprocal(SS[:, 8:16], SS[:, 8:16])
            return eng.drain()
        P.add("vector", em_rec2, writes=(bSS,), tag="srecip")
        P.add("vector", lambda eng: eng.tensor_tensor(
            W8, W8, bcast_ap(SS[:, 8:16], "a"), ALU.mult),
            reads=(bSS,), writes=(bW8,), tag="snorm")
        P.add("vector", lambda eng: eng.tensor_mul(
            W8.rearrange("p a b -> p (a b)"), W8.rearrange("p a b -> p (a b)"),
            HD.rearrange("p a b -> p (a b)")),
            reads=(bHD,), writes=(bW8,), tag="wmul")
        P.add("tensor", lambda eng: eng.transpose(
            ps[2][0:64, 0:128], W8.rearrange("p a b -> p (a b)"), IDENTF),
            reads=(bW8, bIDENTF), writes=(bps[2],), tag="wtr")
        P.add("vector", lambda eng: eng.tensor_copy(WTT, ps[2][0:64, 0:128]),
              reads=(bps[2],), writes=(bWTT,), tag="wtc")
        P.add("tensor", lambda eng: eng.transpose(
            ps[2][0:64, 0:128], IDX8F.rearrange("p a b -> p (a b)"), IDENTF),
            reads=(bIDX8F, bIDENTF), writes=(bps[2],), tag="itr")
        P.add("vector", lambda eng: eng.tensor_copy(IDXT, ps[2][0:64, 0:128]),
              reads=(bps[2],), writes=(bIDXT,), tag="itc")
        P.add("vector", lambda eng: eng.memset(WB2, 0.0), writes=(bWB2,), tag="wbz")
        def em_wbe(eng):
            dst = bass.AP(WB2.tensor, WB2.offset, [[WB2.ap[0][0], 64], [2, 64]])
            src = bass.AP(WTT.tensor, WTT.offset, [WTT.ap[0], [2, 64]])
            return eng.tensor_copy(dst, src)
        P.add("vector", em_wbe, reads=(bWTT,), writes=(bWB2,), tag="wbe")
        def em_wbo(eng):
            d0 = WB2[64:128, :]
            dst = bass.AP(d0.tensor, d0.offset + 1, [[d0.ap[0][0], 64], [2, 64]])
            src = bass.AP(WTT.tensor, WTT.offset + 1, [WTT.ap[0], [2, 64]])
            return eng.tensor_copy(dst, src)
        P.add("vector", em_wbo, reads=(bWTT,), writes=(bWB2,), tag="wbo")
        def em_tie(eng):
            dst = bass.AP(TIDX2.tensor, TIDX2.offset, [[TIDX2.ap[0][0], 64], [1, 64]])
            src = bass.AP(IDXT.tensor, IDXT.offset, [IDXT.ap[0], [2, 64]])
            return eng.tensor_copy(dst, src)
        P.add("vector", em_tie, reads=(bIDXT,), writes=(bTIDX2,), tag="tie")
        def em_tio(eng):
            d0 = TIDX2[64:128, :]
            dst = bass.AP(d0.tensor, d0.offset, [[d0.ap[0][0], 64], [1, 64]])
            src = bass.AP(IDXT.tensor, IDXT.offset + 1, [IDXT.ap[0], [2, 64]])
            return eng.tensor_copy(dst, src)
        P.add("vector", em_tio, reads=(bIDXT,), writes=(bTIDX2,), tag="tio")
        if DBG_DUMP and qt == 0:
            for nm, ap_, b_ in [("d_idx8f", IDX8F.rearrange("p a b -> p (a b)"), bIDX8F),
                                ("d_idxt", IDXT, bIDXT), ("d_tidx2", TIDX2, bTIDX2),
                                ("d_w8", W8.rearrange("p a b -> p (a b)"), bW8),
                                ("d_wb2", WB2, bWB2),
                                ("d_hd", HD.rearrange("p a b -> p (a b)"), bHD),
                                ("d_s8", S8.rearrange("p a b -> p (a b)"), bS8)]:
                P.add("scalar", lambda eng, nm=nm, ap_=ap_: eng.dma_start(
                    out=io[nm].ap(), in_=ap_),
                    reads=(b_,), writes=(io["b_" + nm],), kind="dma", tag="dbg" + nm)
        for c2 in range(64 if S2LVL >= 5 else 0):
            sl = GU[c2 % 4]
            P.add("gpsimd", lambda eng, c2=c2, sl=sl: eng.indirect_dma_start(
                sl[0], None, utab.ap(),
                bass.IndirectOffsetOnAxis(TIDX2[:, c2:c2 + 1], 0)),
                reads=(bTIDX2,), writes=(sl[1],), kind="dma", tag=f"gu{qt}_{c2}")
            for ch in range(8):
                bk = ch // 4
                P.add("tensor", lambda eng, c2=c2, ch=ch, sl=sl, bk=bk: eng.matmul(
                    ps[bk][:, (ch % 4) * 128 + 2 * c2:(ch % 4) * 128 + 2 * c2 + 2],
                    sl[0][:, ch * 128:(ch + 1) * 128], WB2[:, 2 * c2:2 * c2 + 2],
                    start=True, stop=True),
                    reads=(sl[1], bWB2), writes=(bps[bk],), tag="upmm")
        for bk in range(2):
            P.add("vector", lambda eng, bk=bk: eng.tensor_copy(
                OUTF.rearrange("p c t -> p (c t)")[:, bk * 512:(bk + 1) * 512],
                ps[bk][:, 0:512]),
                reads=(bps[bk],), writes=(bOUTF,), tag="upev")
        if DBG_DUMP and qt == 0:
            P.add("scalar", lambda eng: eng.dma_start(
                out=io["d_outf"].ap().rearrange("(c p) t -> p c t", p=128), in_=OUTF),
                reads=(bOUTF,), writes=(io["b_d_outf"],), kind="dma", tag="dbgoutf")
        P.add("vector", lambda eng, XQa=XQa: eng.tensor_add(
            OUTF.rearrange("p c t -> p (c t)"), OUTF.rearrange("p c t -> p (c t)"),
            XQa.rearrange("p c t -> p (c t)")),
            reads=(bXQ,), writes=(bOUTF,), deps=(pfx,), tag="s2res")
        ln_store(P, OUTF, bOUTF, x2d, bx2d, qt, "s2")
        if qt + 1 < QT:
            pfx = nxt

    if S2LVL < 2:
        P.dma("sync", x2d.ap(), x1d.ap(), reads=(bx1d, bQB), writes=(bx2d,), tag="s2byp")
    P.barrier()
    al.reset(mark0)





def bc_mid(ap2d, rep):
    return bass.AP(ap2d.tensor, ap2d.offset, [ap2d.ap[0], [0, rep], ap2d.ap[1]])


def bc_last(ap2d, rep):
    return bass.AP(ap2d.tensor, ap2d.offset, [ap2d.ap[0], ap2d.ap[1], [0, rep]])


def build_stage3(P, al, io, ps, bps, pb, bpb):
    nc = P.nc
    LC = 256
    NLC = TC // LC
    LCC = 512
    NTT = TC // LCC
    mark0 = al.mark()

    CW, bCW = al.take((128, 16, 4), F32, "CW")
    CB, bCB = al.take((128, 16), F32, "CB")
    XPW, bXPW = al.take((128, 16, 96), BF16, "XPW")
    DTW, bDTW = al.take((64, 16, 128), F32, "DTW")
    DTB, bDTB = al.take((128, 16), F32, "DTB")
    AM, bAM = al.take((128, 16, 16), F32, "AM")
    DP, bDP = al.take((128, 16), F32, "DP")
    N1W, bN1W = al.take((128, NCH), F32, "N1W")
    N2W, bN2W = al.take((128, NCH), F32, "N2W")
    HFLG, bHFLG = al.take((128, 256), F32, "HFLG")
    OHB, bOHB = al.take((96, 32), F32, "OHB")
    OHC, bOHC = al.take((32, 16), F32, "OHC")
    ONES1, bONES1 = al.take((128, 1), F32, "ONES1s3")
    ONESR, bONESR = al.take((128, 128), F32, "ONESRs3")
    ONESL, bONESL = al.take((128, LC), F32, "ONESL")
    R1, bR1 = al.take((1, 512), F32, "R1s3")

    P.dma("sync", CW, io["conv_w3"].ap().rearrange("(c p) j -> p c j", p=128),
          writes=(bCW,))
    P.dma("sync", CB, io["conv_b3"].ap(), writes=(bCB,))
    P.dma("gpsimd", XPW, rearr_cp(io["w_xproj"].ap()), writes=(bXPW,))
    P.dma("sync", DTW, io["w_dtproj"].ap().rearrange("p (c m) -> p c m", c=16),
          writes=(bDTW,))
    P.dma("sync", DTB, io["dt_b3"].ap(), writes=(bDTB,))
    P.dma("sync", DP, io["d_vec3"].ap(), writes=(bDP,))
    P.dma("sync", N1W, io["n1w"].ap(), writes=(bN1W,))
    P.dma("sync", N2W, io["n2w"].ap(), writes=(bN2W,))
    P.dma("sync", HFLG, io["hflag3"].ap(), writes=(bHFLG,))
    P.dma("sync", OHB, io["ohb"].ap(), writes=(bOHB,))
    P.dma("sync", OHC, io["ohc"].ap(), writes=(bOHC,))
    P.add("vector", lambda eng: eng.memset(ONES1, 1.0), writes=(bONES1,))
    P.add("vector", lambda eng: eng.memset(ONESR[0:1, :], 1.0), writes=(bONESR,))
    P.add("vector", lambda eng: eng.memset(ONESL, 1.0), writes=(bONESL,))

    x2d, bx2d = io["x2d"], io["b_x2d"]
    x3d, bx3d = io["x3d"], io["b_x3d"]
    yzd, byzd = io["yzd"], Buf("yzd")
    cdsd, bcdsd = io["cdsd"], Buf("cdsd")
    zsd, bzsd = io["zsd"], Buf("zsd")
    bcd, bbcd = io["bcd"], Buf("bcd")
    hd3, bhd3 = io["hd3"], Buf("hd3")
    w1b, bw1b = io["w1b"], Buf("w1b")
    w2b, bw2b = io["w2b"], Buf("w2b")
    w3b, bw3b = io["w3b"], Buf("w3b")

    def rms_make(XT, bXT, WCOL, bWC, OUT, bOUT, SQ, bSQ, LN, bLN, wid, tag):
        for ec in range(NCH):
            P.add("scalar", lambda eng, ec=ec: eng.activation(
                SQ[:, 0:wid], XT[:, ec, 0:wid], ACTF.Square),
                reads=(bXT,), writes=(bSQ,), tag=tag + "sq")
            P.add("tensor", lambda eng, ec=ec: eng.matmul(
                ps[4][0:1, 0:wid], ONES1, SQ[:, 0:wid],
                start=(ec == 0), stop=(ec == NCH - 1)),
                reads=(bONES1, bSQ), writes=(bps[4],), tag=tag + "mm")
        P.add("vector", lambda eng: eng.tensor_scalar(
            LN[0:1, 0:wid], ps[4][0:1, 0:wid], 1.0 / 1024.0, 1e-5,
            ALU.mult, ALU.add),
            reads=(bps[4],), writes=(bLN,), tag=tag + "ms")
        P.add("scalar", lambda eng: eng.activation(
            LN[0:1, 0:wid], LN[0:1, 0:wid], ACTF.Sqrt),
            writes=(bLN,), tag=tag + "sqrt")
        def em_rc(eng):
            eng.reciprocal(LN[0:1, 0:wid], LN[0:1, 0:wid])
            return eng.drain()
        P.add("vector", em_rc, writes=(bLN,), tag=tag + "recip")
        P.add("tensor", lambda eng: eng.matmul(
            ps[4][:, 0:wid], ONESR[0:1, :], LN[0:1, 0:wid],
            start=True, stop=True),
            reads=(bONESR, bLN), writes=(bps[4],), tag=tag + "bc")
        P.add("vector", lambda eng: eng.tensor_copy(LN[:, 0:wid], ps[4][:, 0:wid]),
              reads=(bps[4],), writes=(bLN,), tag=tag + "bcc")
        for ec in range(NCH):
            P.add("vector", lambda eng, ec=ec: eng.tensor_mul(
                SQ[:, 0:wid], XT[:, ec, 0:wid], LN[:, 0:wid]),
                reads=(bXT, bLN), writes=(bSQ,), tag=tag + "mul")
            P.add("scalar", lambda eng, ec=ec: eng.activation(
                OUT[:, ec, 0:wid], SQ[:, 0:wid], ACTF.Identity,
                scale=WCOL[:, ec:ec + 1]),
                reads=(bSQ, bWC), writes=(bOUT,), tag=tag + "sc")

    ubd, bubd = io["ubd"], Buf("ubd")
    markA = al.mark()
    WIN, bWIN = al.take((128, NCH, 4096), BF16, "WIN")
    X2T, bX2T = al.take((128, NCH, 512), F32, "X2T")
    XNB, bXNB = al.take((128, NCH, 512), BF16, "XNB")
    UT = [al.take((128, 512), F32, f"UT{i}") for i in range(2)]
    ZT = [al.take((128, 512), BF16, f"ZT{i}") for i in range(2)]
    LNS, bLNS = al.take((128, 512), F32, "LNSs3")
    SQS, bSQS = al.take((128, 512), F32, "SQSs3")
    ALOG, bALOG = al.take((128, 16, 16), F32, "ALOG")

    P.dma("gpsimd", WIN, rearr_cp(io["w_inproj"].ap()), writes=(bWIN,))
    P.add("sync", lambda eng: eng.dma_start(
        out=ALOG, in_=io["a_log3"].ap().rearrange("(c p) n -> p c n", p=128)),
        writes=(bALOG,), kind="dma", tag="ldalog")
    P.add("scalar", lambda eng: eng.activation(
        AM.rearrange("p a b -> p (a b)"), ALOG.rearrange("p a b -> p (a b)"),
        ACTF.Exp),
        reads=(bALOG,), writes=(bAM,), tag="aexp")
    P.add("vector", lambda eng: eng.tensor_scalar_mul(
        AM.rearrange("p a b -> p (a b)"), AM.rearrange("p a b -> p (a b)"), -1.0),
        writes=(bAM,), tag="aneg")

    for lt in range(TC // 512):
        tsl = slice(lt * 512, (lt + 1) * 512)
        P.add("sync", lambda eng, tsl=tsl: eng.dma_start(
            out=X2T, in_=rearr_cp(x2d.ap())[:, :, tsl]),
            reads=(bx2d,), writes=(bX2T,), kind="dma", tag=f"a1ldx{lt}")
        rms_make(X2T, bX2T, N1W, bN1W, XNB, bXNB, SQS, bSQS, LNS, bLNS,
                 512, f"a1rms{lt}")
        for ch in range(32):
            for c in range(NCH):
                P.add("tensor", lambda eng, ch=ch, c=c: eng.matmul(
                    ps[0][:, 0:512], WIN[:, c, ch * 128:(ch + 1) * 128],
                    XNB[:, c, :], start=(c == 0), stop=(c == NCH - 1)),
                    reads=(bWIN, bXNB), writes=(bps[0],), tag="xzmm")
            if ch < 16:
                ut, but = UT[ch % 2]
                P.add("vector", lambda eng, ut=ut: eng.tensor_copy(
                    ut, ps[0][:, 0:512]),
                    reads=(bps[0],), writes=(but,), tag="uev")
                P.add("sync", lambda eng, ch=ch, ut=ut, tsl=tsl: eng.dma_start(
                    out=ubd.ap().rearrange("(c p) t -> p c t", p=128)[:, ch, tsl],
                    in_=ut),
                    reads=(but,), writes=(bubd,), kind="dma", tag=f"ust{lt}_{ch}")
            else:
                zt, bzt = ZT[ch % 2]
                P.add("scalar", lambda eng, zt=zt: eng.activation(
                    zt, ps[0][:, 0:512], ACTF.Silu),
                    reads=(bps[0],), writes=(bzt,), tag="zev")
                P.add("sync", lambda eng, ch=ch, zt=zt, tsl=tsl: eng.dma_start(
                    out=zsd.ap().rearrange("(c p) t -> p c t", p=128)
                    [:, ch - 16, tsl], in_=zt),
                    reads=(bzt,), writes=(bzsd,), kind="dma", tag=f"zst{lt}_{ch}")

    P.add("sync", lambda eng: eng.dma_start(
        out=io["hb3in"].ap().rearrange("p (c t) -> p c t", c=16),
        in_=ubd.ap().rearrange("(c p) t -> p c t", p=128)[:, :, TC - 3:TC]),
        reads=(bubd,), writes=(io["b_hb3in"],), kind="dma", tag="hb3st")
    P.add("gpsimd", lambda eng: eng.collective_compute(
        "AllGather", mybir.AluOpType.bypass,
        replica_groups=[[0, 1], [2, 3], [4, 5], [6, 7]],
        ins=[io["hb3in"].ap()], outs=[io["hb3out"].ap()]),
        reads=(io["b_hb3in"],), writes=(io["b_hb3out"],), kind="cc", tag="hb3cc")

    P.barrier()
    al.reset(markA)

    UB, bUB = al.take((128, 16, LC + 3), F32, "UB")
    UH0, bUH0 = al.take((128, 16, 3), F32, "UH0")
    UAB, bUAB = al.take((128, 16, LC), BF16, "UAB")
    P96, bP96 = al.take((96, LC), F32, "P96")
    DT3, bDT3 = al.take((128, 16, LC), F32, "DT3")
    DTU3, bDTU3 = al.take((128, 16, LC), F32, "DTU3")
    YA3, bYA3 = al.take((128, 16, LC), F32, "YA3")
    DBU3, bDBU3 = al.take((128, 16, LC), F32, "DBU3")
    H3, bH3 = al.take((128, 16, LC), F32, "H3")
    BRC, bBRC = al.take((128, 2, LC), F32, "BRC")
    DA2 = [al.take((128, LC), F32, f"DA2{i}") for i in range(2)]
    CDT, bCDT = al.take((128, LC), F32, "CDT")
    TMP, bTMP = al.take((128, LC), F32, "TMPs3")
    LNA, bLNA = al.take((128, LC), F32, "LNAs3")
    SQA, bSQA = al.take((128, LC), F32, "SQAs3")
    SCST, bSCST = al.take((128, 256), F32, "SCST")
    CDC, bCDC = al.take((128, 16), F32, "CDC")

    P.add("vector", lambda eng: eng.memset(SCST, 0.0), writes=(bSCST,))
    P.add("vector", lambda eng: eng.memset(CDC, 0.0), writes=(bCDC,))
    P.add("sync", lambda eng: eng.dma_start(
        out=UH0, in_=io["hb3out"].ap()[0:128, :].rearrange(
            "p (c t) -> p c t", c=16)),
        reads=(io["b_hb3out"],), writes=(bUH0,), kind="dma", tag="hb3ld")
    P.add("vector", lambda eng: eng.tensor_tensor(
        UH0, UH0, bc_mid(HFLG[:, 0:3], 16), ALU.mult),
        reads=(bHFLG,), writes=(bUH0,), tag="uhmask")

    for lc in range(NLC):
        tsl = slice(lc * LC, (lc + 1) * LC)
        if lc == 0:
            P.add("sync", lambda eng: eng.dma_start(
                out=UB[:, :, 3:LC + 3],
                in_=ubd.ap().rearrange("(c p) t -> p c t", p=128)[:, :, 0:LC]),
                reads=(bubd,), writes=(bUB,), kind="dma", tag="a2ldu0")
            P.add("vector", lambda eng: eng.tensor_copy(UB[:, :, 0:3], UH0),
                  reads=(bUH0,), writes=(bUB,), tag="uh0cp")
        else:
            P.add("sync", lambda eng, lc=lc: eng.dma_start(
                out=UB,
                in_=ubd.ap().rearrange("(c p) t -> p c t", p=128)
                [:, :, lc * LC - 3:(lc + 1) * LC]),
                reads=(bubd,), writes=(bUB,), kind="dma", tag=f"a2ldu{lc}")
        for ch in range(16):
            P.add("scalar", lambda eng, ch=ch: eng.activation(
                TMP, UB[:, ch, 0:LC], ACTF.Identity, scale=CW[:, ch, 0:1],
                bias=CB[:, ch:ch + 1]),
                reads=(bUB, bCW, bCB), writes=(bTMP,), tag="cv0")
            for j in range(1, 4):
                P.add("vector", lambda eng, ch=ch, j=j: eng.scalar_tensor_tensor(
                    TMP, UB[:, ch, j:j + LC], CW[:, ch, j:j + 1], TMP,
                    ALU.mult, ALU.add),
                    reads=(bUB, bCW), writes=(bTMP,), tag="cvj")
            P.add("scalar", lambda eng, ch=ch: eng.activation(
                YA3[:, ch, :], TMP, ACTF.Silu),
                reads=(bTMP,), writes=(bYA3,), tag="cvs")
            P.add("vector", lambda eng, ch=ch: eng.tensor_copy(
                UAB[:, ch, :], YA3[:, ch, :]),
                reads=(bYA3,), writes=(bUAB,), tag="uab")
        for c in range(16):
            P.add("tensor", lambda eng, c=c: eng.matmul(
                ps[1][0:96, 0:LC], XPW[:, c, :], UAB[:, c, :],
                start=(c == 0), stop=(c == 15)),
                reads=(bXPW, bUAB), writes=(bps[1],), tag="p96mm")
        P.add("vector", lambda eng: eng.tensor_copy(P96, ps[1][0:96, 0:LC]),
              reads=(bps[1],), writes=(bP96,), tag="p96ev")
        P.add("sync", lambda eng, tsl=tsl: eng.dma_start(
            out=bcd.ap()[:, tsl], in_=P96[64:96, :]),
            reads=(bP96,), writes=(bbcd,), kind="dma", tag=f"bcst{lc}")
        for ch in range(16):
            P.add("tensor", lambda eng, ch=ch: eng.matmul(
                ps[2][:, 0:LC], DTW[:, ch, :], P96[0:64, :],
                start=True, stop=True),
                reads=(bDTW, bP96), writes=(bps[2],), tag="dtmm")
            P.add("vector", lambda eng, ch=ch: eng.tensor_scalar(
                DT3[:, ch, :], ps[2][:, 0:LC], DTB[:, ch:ch + 1], None, ALU.add),
                reads=(bps[2], bDTB), writes=(bDT3,), tag="dtb")
            P.add("scalar", lambda eng, ch=ch: eng.activation(
                TMP, DT3[:, ch, :], ACTF.Abs),
                reads=(bDT3,), writes=(bTMP,), tag="spabs")
            P.add("vector", lambda eng: eng.tensor_scalar_mul(TMP, TMP, -1.0),
                  writes=(bTMP,), tag="spneg")
            P.add("scalar", lambda eng: eng.activation(SQA, TMP, ACTF.Exp),
                  reads=(bTMP,), writes=(bSQA,), tag="spexp")
            P.add("scalar", lambda eng: eng.activation(
                TMP, SQA, ACTF.Ln, bias=1.0),
                reads=(bSQA,), writes=(bTMP,), tag="spln")
            P.add("vector", lambda eng, ch=ch: eng.tensor_single_scalar(
                SQA, DT3[:, ch, :], 0.0, ALU.max),
                reads=(bDT3,), writes=(bSQA,), tag="sprelu")
            P.add("vector", lambda eng, ch=ch: eng.tensor_add(
                DT3[:, ch, :], SQA, TMP),
                reads=(bSQA, bTMP), writes=(bDT3,), tag="spadd")
            P.add("vector", lambda eng, ch=ch: eng.tensor_mul(
                DTU3[:, ch, :], DT3[:, ch, :], YA3[:, ch, :]),
                reads=(bDT3, bYA3), writes=(bDTU3,), tag="dtu")
            P.add("scalar", lambda eng, ch=ch: eng.activation(
                TMP, YA3[:, ch, :], ACTF.Identity, scale=DP[:, ch:ch + 1]),
                reads=(bYA3, bDP), writes=(bTMP,), tag="ydmul")
            P.add("vector", lambda eng, ch=ch: eng.tensor_copy(
                YA3[:, ch, :], TMP),
                reads=(bTMP,), writes=(bYA3,), tag="ydcp")
            def em_cum(eng, ch=ch):
                return eng.tensor_tensor_scan(
                    CDT, ONESL, DT3[:, ch, :], CDC[:, ch:ch + 1],
                    ALU.mult, ALU.add)
            P.add("vector", em_cum, reads=(bONESL, bDT3, bCDC), writes=(bCDT,),
                  tag="cum")
            P.add("vector", lambda eng, ch=ch: eng.tensor_copy(
                CDC[:, ch:ch + 1], CDT[:, LC - 1:LC]),
                reads=(bCDT,), writes=(bCDC,), tag="cdcu")
            P.add("sync", lambda eng, ch=ch, tsl=tsl: eng.dma_start(
                out=cdsd.ap().rearrange("(c p) t -> p c t", p=128)[:, ch, tsl],
                in_=CDT),
                reads=(bCDT,), writes=(bcdsd,), kind="dma", tag=f"cdst{lc}_{ch}")
        for n in range(16):
            for half, col in ((0, n), (1, 16 + n)):
                P.add("tensor", lambda eng, col=col: eng.matmul(
                    ps[5][0:1, 0:LC], OHB[:, col:col + 1], P96,
                    start=True, stop=True),
                    reads=(bOHB, bP96), writes=(bps[5],), tag="bcx")
                P.add("vector", lambda eng: eng.tensor_copy(
                    R1[0:1, 0:LC], ps[5][0:1, 0:LC]),
                    reads=(bps[5],), writes=(bR1,), tag="bcxe")
                P.add("tensor", lambda eng: eng.matmul(
                    ps[3][:, 0:LC], ONESR[0:1, :], R1[0:1, 0:LC],
                    start=True, stop=True),
                    reads=(bONESR, bR1), writes=(bps[3],), tag="bcb")
                P.add("vector", lambda eng, half=half: eng.tensor_copy(
                    BRC[:, half, :], ps[3][:, 0:LC]),
                    reads=(bps[3],), writes=(bBRC,), tag="bcbe")
            P.add("vector", lambda eng: eng.tensor_tensor(
                DBU3, DTU3, bc_mid(BRC[:, 0, :], 16), ALU.mult),
                reads=(bDTU3, bBRC), writes=(bDBU3,), tag="dbu")
            for ch in range(16):
                da, bda = DA2[ch % 2]
                P.add("scalar", lambda eng, ch=ch, n=n, da=da: eng.activation(
                    da, DT3[:, ch, :], ACTF.Exp, scale=AM[:, ch, n:n + 1]),
                    reads=(bDT3, bAM), writes=(bda,), tag="dA")
                sc_i = n * 16 + ch
                def em_scan(eng, ch=ch, sc_i=sc_i, da=da):
                    return eng.tensor_tensor_scan(
                        H3[:, ch, :], da, DBU3[:, ch, :],
                        SCST[:, sc_i:sc_i + 1], ALU.mult, ALU.add)
                P.add("vector", em_scan, reads=(bda, bDBU3, bSCST),
                      writes=(bH3,), tag="scan")
                P.add("vector", lambda eng, ch=ch, sc_i=sc_i: eng.tensor_copy(
                    SCST[:, sc_i:sc_i + 1], H3[:, ch, LC - 1:LC]),
                    reads=(bH3,), writes=(bSCST,), tag="scstu")
            P.add("vector", lambda eng: eng.tensor_tensor(
                H3, H3, bc_mid(BRC[:, 1, :], 16), ALU.mult),
                reads=(bBRC,), writes=(bH3,), tag="cmul")
            P.add("vector", lambda eng: eng.tensor_tensor(
                YA3, YA3, H3, ALU.add),
                reads=(bH3,), writes=(bYA3,), tag="yadd")
        P.add("sync", lambda eng, tsl=tsl: eng.dma_start(
            out=yzd.ap().rearrange("(c p) t -> p c t", p=128)[:, :, tsl],
            in_=YA3),
            reads=(bYA3,), writes=(byzd,), kind="dma", tag=f"yst{lc}")

    P.add("sync", lambda eng: eng.dma_start(out=io["hstin"].ap(), in_=SCST),
          reads=(bSCST,), writes=(io["b_hstin"],), kind="dma", tag="hstx")
    P.add("gpsimd", lambda eng: eng.collective_compute(
        "AllGather", mybir.AluOpType.bypass,
        replica_groups=[[0, 1], [2, 3], [4, 5], [6, 7]],
        ins=[io["hstin"].ap()], outs=[io["hstout"].ap()]),
        reads=(io["b_hstin"],), writes=(io["b_hstout"],), kind="cc", tag="hcc")

    P.barrier()
    al.reset(markA)

    OPW, bOPW = al.take((128, 16, 1024), BF16, "OPW")
    HIN, bHIN = al.take((128, 256), F32, "HIN")
    LC1 = 256
    NT1 = TC // LC1
    X2T2, bX2T2 = al.take((128, NCH, LC1), F32, "X2T2")
    YB, bYB = al.take((128, 16, LC1), F32, "YB")
    CDS2, bCDS2 = al.take((128, 16, LC1), F32, "CDS2")
    EX3, bEX3 = al.take((128, 16, LC1), F32, "EX3")
    GB, bGB = al.take((128, 16, LC1), BF16, "GB")
    CR2, bCR2 = al.take((128, LC1), F32, "CR2")
    BC2, bBC2 = al.take((32, LC1), F32, "BC2")
    WCS = [al.take((128, 4096), BF16, f"WCS{i}") for i in range(2)]

    P.dma("gpsimd", OPW, rearr_cp(io["w_outproj"].ap()), writes=(bOPW,))
    P.add("sync", lambda eng: eng.dma_start(
        out=HIN, in_=io["hstout"].ap()[0:128, :]),
        reads=(io["b_hstout"],), writes=(bHIN,), kind="dma", tag="hldx")
    P.add("vector", lambda eng: eng.tensor_mul(HIN, HIN, HFLG),
          reads=(bHFLG,), writes=(bHIN,), tag="hflagm")

    for src, dst, bdst, nch in ((io["w_mlp1"], w1b, bw1b, NCH),
                                (io["w_mlp2"], w2b, bw2b, NCH),
                                (io["w_mlp3"], w3b, bw3b, 32)):
        wid = 4096 if nch == NCH else 1024
        for c in range(nch):
            wc, bwc = WCS[c % 2]
            P.add("gpsimd", lambda eng, src=src, c=c, wc=wc, wid=wid:
                  eng.dma_start(out=wc[:, 0:wid],
                                in_=rearr_cp(src.ap())[:, c, :]),
                  writes=(bwc,), kind="dma", tag=f"pcl{nch}_{c}")
            P.add("gpsimd", lambda eng, dst=dst, c=c, wc=wc, wid=wid:
                  eng.dma_start(out=rearr_cp(dst.ap())[:, c, :],
                                in_=wc[:, 0:wid]),
                  reads=(bwc,), writes=(bdst,), kind="dma", tag=f"pcs{nch}_{c}")

    for tt in range(NT1):
        tsl = slice(tt * LC1, (tt + 1) * LC1)
        P.add("sync", lambda eng, tsl=tsl: eng.dma_start(
            out=X2T2, in_=rearr_cp(x2d.ap())[:, :, tsl]),
            reads=(bx2d,), writes=(bX2T2,), kind="dma", tag=f"cldx{tt}")
        P.add("sync", lambda eng, tsl=tsl: eng.dma_start(
            out=YB, in_=yzd.ap().rearrange("(c p) t -> p c t", p=128)[:, :, tsl]),
            reads=(byzd,), writes=(bYB,), kind="dma", tag=f"cldy{tt}")
        P.add("sync", lambda eng, tsl=tsl: eng.dma_start(
            out=CDS2,
            in_=cdsd.ap().rearrange("(c p) t -> p c t", p=128)[:, :, tsl]),
            reads=(bcdsd,), writes=(bCDS2,), kind="dma", tag=f"cldc{tt}")
        P.add("sync", lambda eng, tsl=tsl: eng.dma_start(
            out=GB, in_=zsd.ap().rearrange("(c p) t -> p c t", p=128)[:, :, tsl]),
            reads=(bzsd,), writes=(bGB,), kind="dma", tag=f"cldz{tt}")
        P.add("sync", lambda eng, tsl=tsl: eng.dma_start(
            out=BC2, in_=bcd.ap()[:, tsl]),
            reads=(bbcd,), writes=(bBC2,), kind="dma", tag=f"cldbc{tt}")
        for n in range(16):
            P.add("tensor", lambda eng, n=n: eng.matmul(
                ps[5][0:1, 0:LC1], OHC[:, n:n + 1], BC2,
                start=True, stop=True),
                reads=(bOHC, bBC2), writes=(bps[5],), tag="c2x")
            P.add("vector", lambda eng: eng.tensor_copy(
                R1[0:1, 0:LC1], ps[5][0:1, 0:LC1]),
                reads=(bps[5],), writes=(bR1,), tag="c2xe")
            P.add("tensor", lambda eng: eng.matmul(
                ps[3][:, 0:LC1], ONESR[0:1, :], R1[0:1, 0:LC1],
                start=True, stop=True),
                reads=(bONESR, bR1), writes=(bps[3],), tag="c2b")
            P.add("vector", lambda eng: eng.tensor_copy(CR2, ps[3][:, 0:LC1]),
                  reads=(bps[3],), writes=(bCR2,), tag="c2be")
            for ch in range(16):
                P.add("scalar", lambda eng, ch=ch, n=n: eng.activation(
                    EX3[:, ch, :], CDS2[:, ch, :], ACTF.Exp,
                    scale=AM[:, ch, n:n + 1]),
                    reads=(bCDS2, bAM), writes=(bEX3,), tag="cexp")
            hsl = HIN[:, n * 16:(n + 1) * 16]
            P.add("vector", lambda eng, hsl=hsl: eng.tensor_tensor(
                EX3, EX3, bc_last(hsl, LC1), ALU.mult),
                reads=(bHIN,), writes=(bEX3,), tag="chin")
            P.add("vector", lambda eng: eng.tensor_tensor(
                EX3, EX3, bc_mid(CR2, 16), ALU.mult),
                reads=(bCR2,), writes=(bEX3,), tag="ccr")
            P.add("vector", lambda eng: eng.tensor_tensor(
                YB, YB, EX3, ALU.add),
                reads=(bEX3,), writes=(bYB,), tag="cadd")
        P.add("vector", lambda eng: eng.tensor_tensor(GB, GB, YB, ALU.mult),
              reads=(bYB,), writes=(bGB,), tag="gmul")
        for e in range(NCH):
            for c in range(16):
                P.add("tensor", lambda eng, e=e, c=c: eng.matmul(
                    ps[0][:, 0:LC1], OPW[:, c, e * 128:(e + 1) * 128],
                    GB[:, c, :], start=(c == 0), stop=(c == 15)),
                    reads=(bOPW, bGB), writes=(bps[0],), tag="opmm")
            P.add("vector", lambda eng, e=e: eng.tensor_add(
                X2T2[:, e, :], X2T2[:, e, :], ps[0][:, 0:LC1]),
                reads=(bps[0],), writes=(bX2T2,), tag="hres")
        P.add("sync", lambda eng, tsl=tsl: eng.dma_start(
            out=rearr_cp(hd3.ap())[:, :, tsl], in_=X2T2),
            reads=(bX2T2,), writes=(bhd3,), kind="dma", tag=f"hst{tt}")

    P.barrier()
    al.reset(markA)

    HB, bHB = al.take((128, NCH, LCC), F32, "HB")
    X2B, bX2B = al.take((128, NCH, LCC), F32, "X2B")
    XN2B, bXN2B = al.take((128, NCH, LCC), BF16, "XN2B")
    MBUF, bMBUF = al.take((128, 32, LCC), BF16, "MBUF")
    W1S = [al.take((128, NCH, 128), BF16, f"W1S{i}") for i in range(2)]
    W2S = [al.take((128, NCH, 128), BF16, f"W2S{i}") for i in range(2)]
    W3S = [al.take((128, LCC), BF16, f"W3S{i}") for i in range(2)]
    TMP2, bTMP2 = al.take((128, LCC), F32, "TMP2s3")
    LNS2, bLNS2 = al.take((128, LCC), F32, "LNS2s3")
    SQS2, bSQS2 = al.take((128, LCC), F32, "SQS2s3")

    def ln_w_store(RESap, bRES, dst, bdst, tt, tag):
        W = LCC
        for ec in range(NCH):
            P.add("tensor", lambda eng, ec=ec: eng.matmul(
                ps[4][0:1, 0:W], ONES1, RESap[:, ec, :],
                start=(ec == 0), stop=(ec == NCH - 1)),
                reads=(bONES1, bRES), writes=(bps[4],), tag=tag + "lsum")
        for ec in range(NCH):
            P.add("scalar", lambda eng, ec=ec: eng.activation(
                SQS2, RESap[:, ec, :], ACTF.Square),
                reads=(bRES,), writes=(bSQS2,), tag=tag + "lsq")
            P.add("tensor", lambda eng, ec=ec: eng.matmul(
                ps[5][0:1, 0:W], ONES1, SQS2,
                start=(ec == 0), stop=(ec == NCH - 1)),
                reads=(bONES1, bSQS2), writes=(bps[5],), tag=tag + "lsumsq")
        P.add("vector", lambda eng: eng.tensor_scalar_mul(
            LNS2[0:1, :], ps[4][0:1, 0:W], 1.0 / 1024.0),
            reads=(bps[4],), writes=(bLNS2,), tag=tag + "lmean")
        P.add("vector", lambda eng: eng.tensor_scalar_mul(
            TMP2[0:1, :], ps[5][0:1, 0:W], 1.0 / 1024.0),
            reads=(bps[5],), writes=(bTMP2,), tag=tag + "lms")
        P.add("vector", lambda eng: eng.tensor_mul(
            SQS2[0:1, :], LNS2[0:1, :], LNS2[0:1, :]),
            reads=(bLNS2,), writes=(bSQS2,), tag=tag + "lm2")
        P.add("vector", lambda eng: eng.scalar_tensor_tensor(
            TMP2[0:1, :], TMP2[0:1, :], 1e-5, SQS2[0:1, :],
            ALU.add, ALU.subtract),
            reads=(bSQS2,), writes=(bTMP2,), tag=tag + "lvar")
        P.add("scalar", lambda eng: eng.activation(
            TMP2[0:1, :], TMP2[0:1, :], ACTF.Sqrt),
            writes=(bTMP2,), tag=tag + "lsqrt")
        def em_rc(eng):
            eng.reciprocal(TMP2[0:1, :], TMP2[0:1, :])
            return eng.drain()
        P.add("vector", em_rc, writes=(bTMP2,), tag=tag + "lrecip")
        P.add("tensor", lambda eng: eng.matmul(
            ps[4][:, 0:W], ONESR[0:1, :], LNS2[0:1, :], start=True, stop=True),
            reads=(bONESR, bLNS2), writes=(bps[4],), tag=tag + "lbm")
        P.add("tensor", lambda eng: eng.matmul(
            ps[5][:, 0:W], ONESR[0:1, :], TMP2[0:1, :], start=True, stop=True),
            reads=(bONESR, bTMP2), writes=(bps[5],), tag=tag + "lbr")
        P.add("vector", lambda eng: eng.tensor_copy(LNS2, ps[4][:, 0:W]),
              reads=(bps[4],), writes=(bLNS2,), tag=tag + "lbmc")
        P.add("vector", lambda eng: eng.tensor_copy(TMP2, ps[5][:, 0:W]),
              reads=(bps[5],), writes=(bTMP2,), tag=tag + "lbrc")
        for ec in range(NCH):
            P.add("vector", lambda eng, ec=ec: eng.tensor_sub(
                RESap[:, ec, :], RESap[:, ec, :], LNS2),
                reads=(bLNS2,), writes=(bRES,), tag=tag + "lsub")
            P.add("vector", lambda eng, ec=ec: eng.tensor_mul(
                RESap[:, ec, :], RESap[:, ec, :], TMP2),
                reads=(bTMP2,), writes=(bRES,), tag=tag + "lmul")
        P.add("sync", lambda eng, tt=tt: eng.dma_start(
            out=rearr_cp(dst.ap())[:, :, tt * LCC:(tt + 1) * LCC], in_=RESap),
            reads=(bRES,), writes=(bdst,), kind="dma", tag=tag + f"st{tt}")

    for tt in range(NTT):
        tsl = slice(tt * LCC, (tt + 1) * LCC)
        P.add("sync", lambda eng, tsl=tsl: eng.dma_start(
            out=HB, in_=rearr_cp(hd3.ap())[:, :, tsl]),
            reads=(bhd3,), writes=(bHB,), kind="dma", tag=f"c2ldh{tt}")
        P.add("sync", lambda eng, tsl=tsl: eng.dma_start(
            out=X2B, in_=rearr_cp(x2d.ap())[:, :, tsl]),
            reads=(bx2d,), writes=(bX2B,), kind="dma", tag=f"c2ldx{tt}")
        rms_make(HB, bHB, N2W, bN2W, XN2B, bXN2B, SQS2, bSQS2, LNS2, bLNS2,
                 LCC, f"rms2_{tt}")
        for e in range(NCH):
            P.add("vector", lambda eng, e=e: eng.tensor_add(
                HB[:, e, :], HB[:, e, :], X2B[:, e, :]),
                reads=(bX2B,), writes=(bHB,), tag="hx2")
        for f in range(32):
            a1, b1 = W1S[f % 2]
            a2, b2 = W2S[f % 2]
            P.add("gpsimd", lambda eng, f=f, a1=a1: eng.dma_start(
                out=a1, in_=rearr_cp(w1b.ap())[:, :, f * 128:(f + 1) * 128]),
                reads=(bw1b,), writes=(b1,), kind="dma", tag=f"ldw1_{tt}_{f}")
            P.add("gpsimd", lambda eng, f=f, a2=a2: eng.dma_start(
                out=a2, in_=rearr_cp(w2b.ap())[:, :, f * 128:(f + 1) * 128]),
                reads=(bw2b,), writes=(b2,), kind="dma", tag=f"ldw2_{tt}_{f}")
            for c in range(NCH):
                P.add("tensor", lambda eng, c=c, a1=a1: eng.matmul(
                    ps[1][:, 0:LCC], a1[:, c, :], XN2B[:, c, :],
                    start=(c == 0), stop=(c == NCH - 1)),
                    reads=(b1, bXN2B), writes=(bps[1],), tag="m1mm")
            for c in range(NCH):
                P.add("tensor", lambda eng, c=c, a2=a2: eng.matmul(
                    ps[2][:, 0:LCC], a2[:, c, :], XN2B[:, c, :],
                    start=(c == 0), stop=(c == NCH - 1)),
                    reads=(b2, bXN2B), writes=(bps[2],), tag="m2mm")
            P.add("scalar", lambda eng: eng.activation(
                TMP2, ps[1][:, 0:LCC], ACTF.Silu),
                reads=(bps[1],), writes=(bTMP2,), tag="m1silu")
            P.add("vector", lambda eng, f=f: eng.tensor_mul(
                MBUF[:, f, :], TMP2, ps[2][:, 0:LCC]),
                reads=(bTMP2, bps[2]), writes=(bMBUF,), tag="mprod")
        for half in range(2):
            for f in range(32):
                a3, b3 = W3S[f % 2]
                P.add("gpsimd", lambda eng, f=f, half=half, a3=a3: eng.dma_start(
                    out=a3,
                    in_=w3b.ap().rearrange("(fo p) e -> p fo e", p=128)
                    [:, f, half * 512:(half + 1) * 512]),
                    reads=(bw3b,), writes=(b3,), kind="dma",
                    tag=f"ldw3_{tt}_{half}_{f}")
                for e4 in range(4):
                    P.add("tensor", lambda eng, f=f, e4=e4, a3=a3: eng.matmul(
                        ps[e4][:, 0:LCC], a3[:, e4 * 128:(e4 + 1) * 128],
                        MBUF[:, f, :], start=(f == 0), stop=(f == 31)),
                        reads=(b3, bMBUF), writes=(bps[e4],), tag="m3mm")
            for e4 in range(4):
                P.add("vector", lambda eng, half=half, e4=e4: eng.tensor_add(
                    HB[:, half * 4 + e4, :], HB[:, half * 4 + e4, :],
                    ps[e4][:, 0:LCC]),
                    reads=(bps[e4],), writes=(bHB,), tag="m3acc")
        ln_w_store(HB, bHB, x3d, bx3d, tt, f"s3ln{tt}")

    P.barrier()
    al.reset(mark0)



def bc_free(ap2d, n):
    return bass.AP(ap2d.tensor, ap2d.offset, [ap2d.ap[0], [0, n]])


def build_stage4(P, al, io, ps, bps, pb, bpb):
    nc = P.nc
    mark0 = al.mark()
    x3d, bx3d = io["x3d"], io["b_x3d"]
    xmd, bxmd = io["xmd"], Buf("xmd")
    xmbd, bxmbd = io["xmbd"], Buf("xmbd")
    impd, bimpd = io["impd"], Buf("impd")
    impall, bimpall = io["impall"], Buf("impall")
    affin, baffin = io["affin"], Buf("affin")
    affout, baffout = io["affout"], Buf("affout")
    bY = io["b_yT"]

    ONES1, bONES1 = al.take((128, 1), F32, "ONES1s4")
    ONESR, bONESR = al.take((128, 128), F32, "ONESRs4")
    R1, bR1 = al.take((1, 512), F32, "R1s4")
    OH8, bOH8 = al.take((8, 8), F32, "OH8")
    IDF, bIDF = al.take((128, 128), F32, "IDFs4")
    PRS, bPRS = al.take((8, 2048), F32, "PRS")
    CEL, bCEL = al.take((1, 32), F32, "CEL")
    NSB, bNSB = al.take((128, 1), F32, "NSB")
    AFB, bAFB = al.take((128, 8), F32, "AFB")
    GB1, bGB1 = al.take((128, 8), F32, "GB1")
    LNSC, bLNSC = al.take((128, 8), F32, "LNSC")
    LNOF, bLNOF = al.take((128, 8), F32, "LNOF")
    GB2, bGB2 = al.take((128, 4), F32, "GB2")
    GB3, bGB3 = al.take((8, 1), F32, "GB3")
    AB, bAB = al.take((1, 8), F32, "ABs4")
    EU, bEU = al.take((1, 8), F32, "EU")
    AW, bAW = al.take((128, 8, 8), F32, "AW")
    B1E0, bB1E0 = al.take((128, 96), F32, "B1E0")
    B2E0, bB2E0 = al.take((128, 24), F32, "B2E0")
    B1E1, bB1E1 = al.take((128, 48), F32, "B1E1")
    B2E1, bB2E1 = al.take((128, 48), F32, "B2E1")
    B3E1, bB3E1 = al.take((128, 24), F32, "B3E1")
    B1E2, bB1E2 = al.take((128, 48), F32, "B1E2")
    B2E2, bB2E2 = al.take((128, 16), F32, "B2E2")
    FLG8, bFLG8 = al.take((1, 8), F32, "FLG8")
    FIX8, bFIX8 = al.take((128, 8), F32, "FIX8")
    FB, bFB = al.take((128, 1), F32, "FBs4")
    FIXG, bFIXG = al.take((128, 8), F32, "FIXG")

    P.add("vector", lambda eng: eng.memset(ONES1, 1.0), writes=(bONES1,))
    P.add("vector", lambda eng: eng.memset(ONESR[0:1, :], 1.0), writes=(bONESR,))
    P.dma("sync", OH8, io["oh8"].ap(), writes=(bOH8,))
    P.dma("sync", IDF, io["identf"].ap(), writes=(bIDF,))
    P.dma("sync", GB1, io["gb1"].ap(), writes=(bGB1,))
    P.dma("sync", LNSC, io["lnsc"].ap(), writes=(bLNSC,))
    P.dma("sync", LNOF, io["lnof"].ap(), writes=(bLNOF,))
    P.dma("sync", GB2, io["gb2"].ap(), writes=(bGB2,))
    P.dma("sync", GB3, io["gb3"].ap(), writes=(bGB3,))
    P.dma("sync", AB, io["ab4"].ap(), writes=(bAB,))
    P.dma("sync", EU, io["eu4"].ap(), writes=(bEU,))
    P.dma("sync", AW, io["aff_w"].ap().rearrange("(c p) e -> p c e", p=128), writes=(bAW,))
    P.dma("sync", B1E0, io["b1e0"].ap(), writes=(bB1E0,))
    P.dma("sync", B2E0, io["b2e0"].ap(), writes=(bB2E0,))
    P.dma("sync", B1E1, io["b1e1"].ap(), writes=(bB1E1,))
    P.dma("sync", B2E1, io["b2e1"].ap(), writes=(bB2E1,))
    P.dma("sync", B3E1, io["b3e1"].ap(), writes=(bB3E1,))
    P.dma("sync", B1E2, io["b1e2"].ap(), writes=(bB1E2,))
    P.dma("sync", B2E2, io["b2e2"].ap(), writes=(bB2E2,))

    WCS4 = [al.take((128, 4096), BF16, f"WCS4_{i}") for i in range(2)]
    WEB = {}
    for nm, cols in [("we01", 4096), ("we02", 1024), ("we11", 2048),
                     ("we12", 2048), ("we13", 1024), ("we21", 3072),
                     ("we22", 1024)]:
        src, dst = io[nm], io[nm + "b"]
        bdst = Buf(nm + "b")
        WEB[nm] = (dst, bdst)
        nblk = src.shape[0] // 128
        for c in range(nblk):
            wc, bwc = WCS4[c % 2]
            P.add("gpsimd", lambda eng, src=src, c=c, wc=wc, cols=cols:
                  eng.dma_start(out=wc[:, 0:cols],
                                in_=rearr_cp(src.ap())[:, c, :]),
                  writes=(bwc,), kind="dma", tag=f"pc4l_{nm}_{c}")
            P.add("gpsimd", lambda eng, dst=dst, c=c, wc=wc, cols=cols:
                  eng.dma_start(out=rearr_cp(dst.ap())[:, c, :],
                                in_=wc[:, 0:cols]),
                  reads=(bwc,), writes=(bdst,), kind="dma", tag=f"pc4s_{nm}_{c}")

    markP = al.mark()
    X3T, bX3T = al.take((128, NCH, 512), F32, "X3Ts4")
    SQ, bSQ = al.take((128, 512), F32, "SQs4")
    IMP, bIMP = al.take((1, 512), F32, "IMPs4")
    for t in range(4):
        tsl = slice(t * 512, (t + 1) * 512)
        P.add("sync", lambda eng, tsl=tsl: eng.dma_start(
            out=X3T, in_=rearr_cp(x3d.ap())[:, :, tsl]),
            reads=(bx3d,), writes=(bX3T,), kind="dma", tag=f"p1ldx{t}")
        for ec in range(NCH):
            P.add("scalar", lambda eng, ec=ec: eng.activation(
                SQ, X3T[:, ec, :], ACTF.Square),
                reads=(bX3T,), writes=(bSQ,), tag="p1sq")
            P.add("tensor", lambda eng, ec=ec: eng.matmul(
                ps[4][0:1, 0:512], ONES1, SQ, start=(ec == 0), stop=(ec == NCH - 1)),
                reads=(bONES1, bSQ), writes=(bps[4],), tag="p1mm")
        P.add("scalar", lambda eng: eng.activation(
            IMP, ps[4][0:1, 0:512], ACTF.Sqrt),
            reads=(bps[4],), writes=(bIMP,), tag="p1sqrt")
        P.add("sync", lambda eng, tsl=tsl: eng.dma_start(
            out=impd.ap()[0:1, tsl], in_=IMP),
            reads=(bIMP,), writes=(bimpd,), kind="dma", tag=f"p1st{t}")
    P.add("gpsimd", lambda eng: eng.collective_compute(
        "AllGather", mybir.AluOpType.bypass, replica_groups=[list(range(8))],
        ins=[impd.ap()], outs=[impall.ap()]),
        reads=(bimpd,), writes=(bimpall,), kind="cc", tag="impcc")

    IMPA, bIMPA = al.take((128, 128), F32, "IMPA")
    MS, bMS = al.take((128, 128), F32, "MSs4")
    TT1, bTT1 = al.take((128, 128), F32, "TT1s4")
    MX8, bMX8 = al.take((128, 8), F32, "MX8s4")
    COLB, bCOLB = al.take((128, 1), F32, "COLBs4")
    ROW8, bROW8 = al.take((1, 8), F32, "ROW8s4")
    P.add("sync", lambda eng: eng.dma_start(
        out=IMPA.rearrange("p (a b) -> p a b", a=8),
        in_=impall.ap().rearrange("a (p b) -> p a b", p=128)),
        reads=(bimpall,), writes=(bIMPA,), kind="dma", tag="p2ld")
    LO = CEL[0:1, 0:1]; HI = CEL[0:1, 1:2]; MID = CEL[0:1, 2:3]
    CNT = CEL[0:1, 3:4]; CND = CEL[0:1, 4:5]; CT = CEL[0:1, 5:6]
    ALO = CEL[0:1, 6:7]; AHI = CEL[0:1, 7:8]; THR = CEL[0:1, 8:9]
    MU = CEL[0:1, 9:10]; STD = CEL[0:1, 10:11]; TEMP = CEL[0:1, 11:12]
    NCD = CEL[0:1, 12:13]

    def rowmax_global(SRC, bSRC, dstcell, tag, negate=False):
        def em_mx(eng):
            eng.max(MX8, SRC)
            return eng.drain()
        P.add("vector", em_mx, reads=(bSRC,), writes=(bMX8,), tag=tag + "rm")
        P.add("tensor", lambda eng: eng.transpose(ps[3][0:8, 0:128], MX8, IDF),
              reads=(bMX8, bIDF), writes=(bps[3],), tag=tag + "tr")
        P.add("vector", lambda eng: eng.tensor_copy(R1[0:1, 0:128], ps[3][0:1, 0:128]),
              reads=(bps[3],), writes=(bR1,), tag=tag + "ev")
        def em_mx2(eng):
            eng.max(ROW8, R1[0:1, 0:128])
            return eng.drain()
        P.add("vector", em_mx2, reads=(bR1,), writes=(bROW8,), tag=tag + "rm2")
        if negate:
            P.add("vector", lambda eng: eng.tensor_scalar_mul(
                dstcell, ROW8[0:1, 0:1], -1.0),
                reads=(bROW8,), writes=(bCEL,), tag=tag + "neg")
        else:
            P.add("vector", lambda eng: eng.tensor_copy(dstcell, ROW8[0:1, 0:1]),
                  reads=(bROW8,), writes=(bCEL,), tag=tag + "cp")

    P.add("vector", lambda eng: eng.memset(CEL, 0.0), writes=(bCEL,))
    rowmax_global(IMPA, bIMPA, CT, "p2mx")
    P.add("vector", lambda eng: eng.tensor_scalar(
        HI, CT, 1.0, None, ALU.add), writes=(bCEL,), tag="p2hi")

    def bcast_cell(cell, dst, bdst, tag, wid=1):
        P.add("tensor", lambda eng: eng.matmul(
            ps[3][:, 0:wid], ONESR[0:1, :], cell if wid == 1 else cell,
            start=True, stop=True),
            reads=(bONESR, bCEL), writes=(bps[3],), tag=tag + "bm")
        P.add("vector", lambda eng: eng.tensor_copy(dst, ps[3][:, 0:wid]),
              reads=(bps[3],), writes=(bdst,), tag=tag + "be")

    for it in range(36):
        P.add("vector", lambda eng: eng.tensor_add(MID, LO, HI),
              writes=(bCEL,), tag="bimid0")
        P.add("vector", lambda eng: eng.tensor_scalar_mul(MID, MID, 0.5),
              writes=(bCEL,), tag="bimid1")
        bcast_cell(MID, COLB, bCOLB, f"bib{it}")
        P.add("vector", lambda eng: eng.tensor_tensor(
            MS, IMPA, bc_free(COLB, 128), ALU.is_lt),
            reads=(bIMPA, bCOLB), writes=(bMS,), tag="bicmp")
        P.add("tensor", lambda eng: eng.matmul(
            ps[3][0:1, 0:128], ONES1, MS, start=True, stop=True),
            reads=(bONES1, bMS), writes=(bps[3],), tag="bisum")
        P.add("vector", lambda eng: eng.tensor_copy(R1[0:1, 0:128], ps[3][0:1, 0:128]),
              reads=(bps[3],), writes=(bR1,), tag="bisev")
        P.add("vector", lambda eng: eng.reduce_sum(CNT, R1[0:1, 0:128], axis=AX.X),
              reads=(bR1,), writes=(bCEL,), tag="bired")
        P.add("vector", lambda eng: eng.tensor_single_scalar(
            CND, CNT, 14745.0, ALU.is_ge),
            writes=(bCEL,), tag="bicond")
        P.add("vector", lambda eng: eng.tensor_scalar(
            NCD, CND, -1.0, 1.0, ALU.mult, ALU.add),
            writes=(bCEL,), tag="bincd")
        P.add("vector", lambda eng: eng.tensor_sub(CT, MID, HI),
              writes=(bCEL,), tag="bidh")
        P.add("vector", lambda eng: eng.scalar_tensor_tensor(
            HI, CT, CND, HI, ALU.mult, ALU.add),
            writes=(bCEL,), tag="biuh")
        P.add("vector", lambda eng: eng.tensor_sub(CT, MID, LO),
              writes=(bCEL,), tag="bidl")
        P.add("vector", lambda eng: eng.scalar_tensor_tensor(
            LO, CT, NCD, LO, ALU.mult, ALU.add),
            writes=(bCEL,), tag="biul")
    bcast_cell(HI, COLB, bCOLB, "p2v1")
    BIG = 1.0e30
    P.add("vector", lambda eng: eng.tensor_tensor(
        MS, IMPA, bc_free(COLB, 128), ALU.is_lt),
        reads=(bIMPA, bCOLB), writes=(bMS,), tag="aloc")
    P.add("vector", lambda eng: eng.tensor_scalar(
        TT1, MS, BIG, -BIG, ALU.mult, ALU.add),
        reads=(bMS,), writes=(bTT1,), tag="alop")
    P.add("vector", lambda eng: eng.tensor_mul(MS, IMPA, MS),
          reads=(bIMPA,), writes=(bMS,), tag="alom")
    P.add("vector", lambda eng: eng.tensor_add(MS, MS, TT1),
          reads=(bTT1,), writes=(bMS,), tag="aloa")
    rowmax_global(MS, bMS, ALO, "p2lo")
    P.add("vector", lambda eng: eng.tensor_tensor(
        MS, IMPA, bc_free(COLB, 128), ALU.is_ge),
        reads=(bIMPA, bCOLB), writes=(bMS,), tag="ahic")
    P.add("vector", lambda eng: eng.tensor_scalar(
        TT1, MS, BIG, -BIG, ALU.mult, ALU.add),
        reads=(bMS,), writes=(bTT1,), tag="ahip")
    P.add("vector", lambda eng: eng.tensor_mul(MS, IMPA, MS),
          reads=(bIMPA,), writes=(bMS,), tag="ahim")
    P.add("vector", lambda eng: eng.tensor_scalar_mul(MS, MS, -1.0),
          writes=(bMS,), tag="ahin")
    P.add("vector", lambda eng: eng.tensor_add(MS, MS, TT1),
          reads=(bTT1,), writes=(bMS,), tag="ahia")
    rowmax_global(MS, bMS, AHI, "p2hi2", negate=True)
    import numpy as _np4
    _q = _np4.float32((1.0 - 0.1) * 100.0) / _np4.float32(100.0)
    _qq = _q * _np4.float32(16383.0)
    _hw = float(_qq - _np4.floor(_qq))
    _lw = float(_np4.float32(1.0) - (_qq - _np4.floor(_qq)))
    P.add("vector", lambda eng: eng.tensor_scalar_mul(CT, ALO, _lw),
          writes=(bCEL,), tag="thr0")
    P.add("vector", lambda eng: eng.tensor_scalar_mul(THR, AHI, _hw),
          writes=(bCEL,), tag="thr1")
    P.add("vector", lambda eng: eng.tensor_add(THR, THR, CT),
          writes=(bCEL,), tag="thr2")
    P.add("tensor", lambda eng: eng.matmul(
        ps[3][0:1, 0:128], ONES1, IMPA, start=True, stop=True),
        reads=(bONES1, bIMPA), writes=(bps[3],), tag="sd0")
    P.add("vector", lambda eng: eng.tensor_copy(R1[0:1, 0:128], ps[3][0:1, 0:128]),
          reads=(bps[3],), writes=(bR1,), tag="sd1")
    P.add("vector", lambda eng: eng.reduce_sum(MU, R1[0:1, 0:128], axis=AX.X),
          reads=(bR1,), writes=(bCEL,), tag="sd2")
    P.add("vector", lambda eng: eng.tensor_scalar_mul(MU, MU, 1.0 / 16384.0),
          writes=(bCEL,), tag="sd3")
    bcast_cell(MU, COLB, bCOLB, "sd4")
    P.add("vector", lambda eng: eng.tensor_tensor(
        MS, IMPA, bc_free(COLB, 128), ALU.subtract),
        reads=(bIMPA, bCOLB), writes=(bMS,), tag="sd5")
    P.add("scalar", lambda eng: eng.activation(MS, MS, ACTF.Square),
          writes=(bMS,), tag="sd6")
    P.add("tensor", lambda eng: eng.matmul(
        ps[3][0:1, 0:128], ONES1, MS, start=True, stop=True),
        reads=(bONES1, bMS), writes=(bps[3],), tag="sd7")
    P.add("vector", lambda eng: eng.tensor_copy(R1[0:1, 0:128], ps[3][0:1, 0:128]),
          reads=(bps[3],), writes=(bR1,), tag="sd8")
    P.add("vector", lambda eng: eng.reduce_sum(STD, R1[0:1, 0:128], axis=AX.X),
          reads=(bR1,), writes=(bCEL,), tag="sd9")
    P.add("vector", lambda eng: eng.tensor_scalar_mul(STD, STD, 1.0 / 16384.0),
          writes=(bCEL,), tag="sd10")
    P.add("scalar", lambda eng: eng.activation(STD, STD, ACTF.Sqrt),
          writes=(bCEL,), tag="sd11")
    P.add("vector", lambda eng: eng.tensor_scalar(
        TEMP, STD, 0.1, 1.0, ALU.mult, ALU.add),
        writes=(bCEL,), tag="sd12")

    P.add("vector", lambda eng: eng.tensor_scalar(
        ROW8, EU, 1e-8, None, ALU.add),
        reads=(bEU,), writes=(bROW8,), tag="ns0")
    P.add("vector", lambda eng: eng.tensor_single_scalar(
        FLG8, ROW8, 0.0, ALU.is_gt),
        reads=(bROW8,), writes=(bFLG8,), tag="nsf0")
    P.add("vector", lambda eng: eng.reduce_sum(CT, FLG8, axis=AX.X),
          reads=(bFLG8,), writes=(bCEL,), tag="nsf1")
    P.add("vector", lambda eng: eng.tensor_single_scalar(
        NCD, CT, 7.5, ALU.is_ge),
        writes=(bCEL,), tag="nsf2")
    bcast_cell(NCD, FB, bFB, "nsf3")
    P.add("vector", lambda eng: eng.memset(FIX8, 0.0), writes=(bFIX8,))
    P.add("vector", lambda eng: eng.memset(FIX8[:, 0:2], 0.125), writes=(bFIX8,))
    P.add("vector", lambda eng: eng.tensor_scalar(
        COLB, FB, -1.0, 1.0, ALU.mult, ALU.add),
        reads=(bFB,), writes=(bCOLB,), tag="nsf4")
    P.add("vector", lambda eng: eng.tensor_tensor(
        FIXG, FIX8, bc_free(COLB, 8), ALU.mult),
        reads=(bFIX8, bCOLB), writes=(bFIXG,), tag="nsf5")
    P.add("vector", lambda eng: eng.tensor_single_scalar(
        ROW8, ROW8, 1e-8, ALU.max),
        writes=(bROW8,), tag="nsclamp")
    P.add("scalar", lambda eng: eng.activation(ROW8, ROW8, ACTF.Ln),
          writes=(bROW8,), tag="ns1")
    P.add("vector", lambda eng: eng.tensor_mul(ROW8, ROW8, EU),
          reads=(bEU,), writes=(bROW8,), tag="ns2")
    P.add("vector", lambda eng: eng.reduce_sum(CT, ROW8, axis=AX.X),
          reads=(bROW8,), writes=(bCEL,), tag="ns3")
    P.add("vector", lambda eng: eng.tensor_scalar(
        CT, CT, 0.01, 0.02, ALU.mult, ALU.add),
        writes=(bCEL,), tag="ns4")
    P.add("vector", lambda eng: eng.tensor_single_scalar(
        CT, CT, 1e-4, ALU.max),
        writes=(bCEL,), tag="ns5")
    bcast_cell(CT, NSB, bNSB, "ns6")

    XMF, bXMF = al.take((128, NCH, 512), F32, "XMF")
    XMB, bXMB = al.take((128, NCH, 512), BF16, "XMBs4")
    MSKB, bMSKB = al.take((128, 512), F32, "MSKB")
    SCB, bSCB = al.take((128, 512), F32, "SCBs4")
    ABT, bABT = al.take((128, 512), F32, "ABTs4")
    NRM, bNRM = al.take((1, 512), F32, "NRMs4")
    AFFS, bAFFS = al.take((128, 8), F32, "AFFS")
    AFR, bAFR = al.take((1, 8), F32, "AFRs4")
    P.add("vector", lambda eng: eng.memset(AFFS, 0.0), writes=(bAFFS,))
    for t in range(4):
        tsl = slice(t * 512, (t + 1) * 512)
        P.add("sync", lambda eng, tsl=tsl: eng.dma_start(
            out=X3T, in_=rearr_cp(x3d.ap())[:, :, tsl]),
            reads=(bx3d,), writes=(bX3T,), kind="dma", tag=f"p3ldx{t}")
        P.add("sync", lambda eng, tsl=tsl: eng.dma_start(
            out=IMP, in_=impd.ap()[0:1, tsl]),
            reads=(bimpd,), writes=(bIMP,), kind="dma", tag=f"p3ldi{t}")
        P.add("vector", lambda eng: eng.tensor_tensor(
            R1[0:1, 0:512], IMP, bc_free(THR, 512), ALU.is_gt),
            reads=(bIMP, bCEL), writes=(bR1,), tag="p3msk")
        P.add("tensor", lambda eng: eng.matmul(
            ps[3][:, 0:512], ONESR[0:1, :], R1[0:1, 0:512], start=True, stop=True),
            reads=(bONESR, bR1), writes=(bps[3],), tag="p3mb")
        P.add("vector", lambda eng: eng.tensor_copy(MSKB, ps[3][:, 0:512]),
              reads=(bps[3],), writes=(bMSKB,), tag="p3mbe")
        for ec in range(NCH):
            P.add("vector", lambda eng, ec=ec: eng.tensor_mul(
                XMF[:, ec, :], X3T[:, ec, :], MSKB),
                reads=(bX3T, bMSKB), writes=(bXMF,), tag="p3spk")
            P.add("scalar", lambda eng, ec=ec: eng.activation(
                ABT, XMF[:, ec, :], ACTF.Abs),
                reads=(bXMF,), writes=(bABT,), tag="p3abs")
            P.add("tensor", lambda eng, ec=ec: eng.matmul(
                ps[4][0:1, 0:512], ONES1, ABT, start=(ec == 0), stop=(ec == NCH - 1)),
                reads=(bONES1, bABT), writes=(bps[4],), tag="p3nmm")
        P.add("vector", lambda eng: eng.tensor_scalar(
            NRM, ps[4][0:1, 0:512], 1e-8, None, ALU.add),
            reads=(bps[4],), writes=(bNRM,), tag="p3nrm")
        P.add("vector", lambda eng: eng.tensor_scalar(
            NRM, NRM, TEMP, None, ALU.mult),
            reads=(bCEL,), writes=(bNRM,), tag="p3tmp")
        def em_rcn(eng):
            eng.reciprocal(NRM, NRM)
            return eng.drain()
        P.add("vector", em_rcn, writes=(bNRM,), tag="p3rc")
        P.add("tensor", lambda eng: eng.matmul(
            ps[3][:, 0:512], ONESR[0:1, :], NRM, start=True, stop=True),
            reads=(bONESR, bNRM), writes=(bps[3],), tag="p3sb")
        P.add("vector", lambda eng: eng.tensor_copy(SCB, ps[3][:, 0:512]),
              reads=(bps[3],), writes=(bSCB,), tag="p3sbe")
        for ec in range(NCH):
            P.add("vector", lambda eng, ec=ec: eng.tensor_mul(
                XMF[:, ec, :], XMF[:, ec, :], SCB),
                reads=(bSCB,), writes=(bXMF,), tag="p3nm")
            P.add("vector", lambda eng, ec=ec: eng.tensor_copy(
                XMB[:, ec, :], XMF[:, ec, :]),
                writes=(bXMB,), tag="p3bf")
            P.add("scalar", lambda eng, ec=ec: eng.activation(
                ABT, XMF[:, ec, :], ACTF.Identity, accum_out=COLB),
                reads=(bXMF,), writes=(bABT, bCOLB), tag="p3acc")
            P.add("vector", lambda eng, ec=ec: eng.tensor_add(
                AFFS[:, ec:ec + 1], AFFS[:, ec:ec + 1], COLB),
                reads=(bCOLB,), writes=(bAFFS,), tag="p3aff")
        P.add("sync", lambda eng, tsl=tsl: eng.dma_start(
            out=rearr_cp(xmd.ap())[:, :, tsl], in_=XMF),
            reads=(bXMF,), writes=(bxmd,), kind="dma", tag=f"p3stf{t}")
        P.add("sync", lambda eng, tsl=tsl: eng.dma_start(
            out=rearr_cp(xmbd.ap())[:, :, tsl], in_=XMB),
            reads=(bXMB,), writes=(bxmbd,), kind="dma", tag=f"p3stb{t}")
    P.add("sync", lambda eng: eng.dma_start(out=affin.ap(), in_=AFFS),
          reads=(bAFFS,), writes=(baffin,), kind="dma", tag="affst")
    P.add("gpsimd", lambda eng: eng.collective_compute(
        "AllReduce", mybir.AluOpType.add,
        replica_groups=[[0, 1], [2, 3], [4, 5], [6, 7]],
        ins=[affin.ap()], outs=[affout.ap()]),
        reads=(baffin,), writes=(baffout,), kind="cc", tag="affcc")
    P.add("sync", lambda eng: eng.dma_start(out=AFFS, in_=affout.ap()),
          reads=(baffout,), writes=(bAFFS,), kind="dma", tag="affld")
    P.add("vector", lambda eng: eng.tensor_scalar_mul(AFFS, AFFS, 1.0 / 4096.0),
          writes=(bAFFS,), tag="affsc")
    for c in range(NCH):
        P.add("tensor", lambda eng, c=c: eng.matmul(
            ps[3][0:1, 0:8], AFFS[:, c:c + 1], AW[:, c, :],
            start=(c == 0), stop=(c == NCH - 1)),
            reads=(bAFFS, bAW), writes=(bps[3],), tag="affmm")
    P.add("vector", lambda eng: eng.tensor_copy(AFR, ps[3][0:1, 0:8]),
          reads=(bps[3],), writes=(bAFR,), tag="affev")
    P.add("vector", lambda eng: eng.tensor_add(AFR, AFR, AB),
          reads=(bAB,), writes=(bAFR,), tag="affab")
    P.add("vector", lambda eng: eng.tensor_scalar_mul(AFR, AFR, 0.3),
          writes=(bAFR,), tag="aff03")
    P.add("tensor", lambda eng: eng.matmul(
        ps[3][:, 0:8], ONESR[0:1, :], AFR, start=True, stop=True),
        reads=(bONESR, bAFR), writes=(bps[3],), tag="affbm")
    P.add("vector", lambda eng: eng.tensor_copy(AFB, ps[3][:, 0:8]),
          reads=(bps[3],), writes=(bAFB,), tag="affbe")

    P.barrier()
    al.reset(markP)

    GW1, bGW1 = al.take((128, NCH, 1024), BF16, "GW1")
    GW2, bGW2 = al.take((128, NCH, 512), BF16, "GW2")
    GW3, bGW3 = al.take((128, 4, 8), BF16, "GW3")
    XMB4, bXMB4 = al.take((128, NCH, 512), BF16, "XMB4")
    H1F, bH1F = al.take((128, NCH, 512), F32, "H1F")
    H1B, bH1B = al.take((128, NCH, 512), BF16, "H1B")
    H2B, bH2B = al.take((128, 4, 512), BF16, "H2B")
    SQ4, bSQ4 = al.take((128, 512), F32, "SQ4")
    LG8, bLG8 = al.take((8, 512), F32, "LG8")
    MEANB, bMEANB = al.take((128, 512), F32, "MEANB")
    RSTDB, bRSTDB = al.take((128, 512), F32, "RSTDB")
    NZT, bNZT = al.take((128, 16, 8), F32, "NZT")
    TL, bTL = al.take((128, 8), F32, "TLs4")
    MXT, bMXT = al.take((128, 8), F32, "MXTs4")
    MSK8, bMSK8 = al.take((128, 8), F32, "MSK8")
    NM, bNM = al.take((128, 1), F32, "NMs4")
    E8, bE8 = al.take((128, 8), F32, "E8s4")
    DEN, bDEN = al.take((128, 1), F32, "DENs4")
    RW1, bRW1 = al.take((1, 512), F32, "RW1s4")

    P.dma("gpsimd", GW1, rearr_cp(io["gate_w1"].ap()), writes=(bGW1,))
    P.dma("gpsimd", GW2, rearr_cp(io["gate_w2"].ap()), writes=(bGW2,))
    P.dma("gpsimd", GW3, rearr_cp(io["gate_w3"].ap()), writes=(bGW3,))
    P.add("sync", lambda eng: eng.dma_start(
        out=NZT, in_=io["noiz"].ap().rearrange("(a p) e -> p a e", p=128)),
        writes=(bNZT,), kind="dma", tag="nzld")

    for t in range(4):
        tsl = slice(t * 512, (t + 1) * 512)
        P.add("sync", lambda eng, tsl=tsl: eng.dma_start(
            out=XMB4, in_=rearr_cp(xmbd.ap())[:, :, tsl]),
            reads=(bxmbd,), writes=(bXMB4,), kind="dma", tag=f"g4ldx{t}")
        for g in range(NCH):
            for c in range(NCH):
                P.add("tensor", lambda eng, g=g, c=c: eng.matmul(
                    ps[1][:, 0:512], GW1[:, c, g * 128:(g + 1) * 128],
                    XMB4[:, c, :], start=(c == 0), stop=(c == NCH - 1)),
                    reads=(bGW1, bXMB4), writes=(bps[1],), tag="g1mm")
            P.add("vector", lambda eng, g=g: eng.tensor_scalar(
                H1F[:, g, :], ps[1][:, 0:512], GB1[:, g:g + 1], None, ALU.add),
                reads=(bps[1], bGB1), writes=(bH1F,), tag="g1b")
            P.add("scalar", lambda eng, g=g: eng.activation(
                SQ4, H1F[:, g, :], ACTF.Silu),
                reads=(bH1F,), writes=(bSQ4,), tag="g1silu")
            P.add("vector", lambda eng, g=g: eng.tensor_copy(
                H1F[:, g, :], SQ4),
                reads=(bSQ4,), writes=(bH1F,), tag="g1cp")
        for g in range(NCH):
            P.add("tensor", lambda eng, g=g: eng.matmul(
                ps[4][0:1, 0:512], ONES1, H1F[:, g, :],
                start=(g == 0), stop=(g == NCH - 1)),
                reads=(bONES1, bH1F), writes=(bps[4],), tag="glsum")
        for g in range(NCH):
            P.add("scalar", lambda eng, g=g: eng.activation(
                SQ4, H1F[:, g, :], ACTF.Square),
                reads=(bH1F,), writes=(bSQ4,), tag="glsq")
            P.add("tensor", lambda eng, g=g: eng.matmul(
                ps[5][0:1, 0:512], ONES1, SQ4,
                start=(g == 0), stop=(g == NCH - 1)),
                reads=(bONES1, bSQ4), writes=(bps[5],), tag="glsumsq")
        P.add("vector", lambda eng: eng.tensor_scalar_mul(
            RW1[0:1, :], ps[4][0:1, 0:512], 1.0 / 1024.0),
            reads=(bps[4],), writes=(bRW1,), tag="glmean")
        P.add("vector", lambda eng: eng.tensor_scalar_mul(
            R1[0:1, :], ps[5][0:1, 0:512], 1.0 / 1024.0),
            reads=(bps[5],), writes=(bR1,), tag="glms")
        P.add("vector", lambda eng: eng.tensor_mul(
            SQ4[0:1, 0:512], RW1[0:1, :], RW1[0:1, :]),
            reads=(bRW1,), writes=(bSQ4,), tag="glm2")
        P.add("vector", lambda eng: eng.scalar_tensor_tensor(
            R1[0:1, :], R1[0:1, :], 1e-5, SQ4[0:1, 0:512], ALU.add, ALU.subtract),
            reads=(bSQ4,), writes=(bR1,), tag="glvar")
        P.add("scalar", lambda eng: eng.activation(R1[0:1, :], R1[0:1, :], ACTF.Sqrt),
              writes=(bR1,), tag="glsqrt")
        def em_rcg(eng):
            eng.reciprocal(R1[0:1, :], R1[0:1, :])
            return eng.drain()
        P.add("vector", em_rcg, writes=(bR1,), tag="glrc")
        P.add("tensor", lambda eng: eng.matmul(
            ps[4][:, 0:512], ONESR[0:1, :], RW1[0:1, :], start=True, stop=True),
            reads=(bONESR, bRW1), writes=(bps[4],), tag="glbm")
        P.add("tensor", lambda eng: eng.matmul(
            ps[5][:, 0:512], ONESR[0:1, :], R1[0:1, :], start=True, stop=True),
            reads=(bONESR, bR1), writes=(bps[5],), tag="glbr")
        P.add("vector", lambda eng: eng.tensor_copy(MEANB, ps[4][:, 0:512]),
              reads=(bps[4],), writes=(bMEANB,), tag="glbmc")
        P.add("vector", lambda eng: eng.tensor_copy(RSTDB, ps[5][:, 0:512]),
              reads=(bps[5],), writes=(bRSTDB,), tag="glbrc")
        for g in range(NCH):
            P.add("vector", lambda eng, g=g: eng.tensor_sub(
                SQ4, H1F[:, g, :], MEANB),
                reads=(bH1F, bMEANB), writes=(bSQ4,), tag="glsub")
            P.add("vector", lambda eng, g=g: eng.tensor_mul(
                SQ4, SQ4, RSTDB),
                reads=(bRSTDB,), writes=(bSQ4,), tag="glmul")
            P.add("scalar", lambda eng, g=g: eng.activation(
                H1B[:, g, :], SQ4, ACTF.Identity, scale=LNSC[:, g:g + 1],
                bias=LNOF[:, g:g + 1]),
                reads=(bSQ4, bLNSC, bLNOF), writes=(bH1B,), tag="glso")
        for g2 in range(4):
            for c in range(NCH):
                P.add("tensor", lambda eng, g2=g2, c=c: eng.matmul(
                    ps[1][:, 0:512], GW2[:, c, g2 * 128:(g2 + 1) * 128],
                    H1B[:, c, :], start=(c == 0), stop=(c == NCH - 1)),
                    reads=(bGW2, bH1B), writes=(bps[1],), tag="g2mm")
            P.add("vector", lambda eng, g2=g2: eng.tensor_scalar(
                SQ4, ps[1][:, 0:512], GB2[:, g2:g2 + 1], None, ALU.add),
                reads=(bps[1], bGB2), writes=(bSQ4,), tag="g2b")
            P.add("scalar", lambda eng, g2=g2: eng.activation(
                H2B[:, g2, :], SQ4, ACTF.Silu),
                reads=(bSQ4,), writes=(bH2B,), tag="g2silu")
        for g2 in range(4):
            P.add("tensor", lambda eng, g2=g2: eng.matmul(
                ps[2][0:8, 0:512], GW3[:, g2, :], H2B[:, g2, :],
                start=(g2 == 0), stop=(g2 == 3)),
                reads=(bGW3, bH2B), writes=(bps[2],), tag="g3mm")
        P.add("vector", lambda eng: eng.tensor_scalar(
            LG8, ps[2][0:8, 0:512], GB3, None, ALU.add),
            reads=(bps[2], bGB3), writes=(bLG8,), tag="g3b")
        for i in range(4):
            ti = t * 4 + i
            P.add("tensor", lambda eng, i=i: eng.transpose(
                ps[3][:, 0:8], LG8[:, i * 128:(i + 1) * 128], OH8),
                reads=(bLG8, bOH8), writes=(bps[3],), tag="gtr")
            P.add("vector", lambda eng: eng.tensor_copy(TL, ps[3][:, 0:8]),
                  reads=(bps[3],), writes=(bTL,), tag="gtre")
            P.add("vector", lambda eng: eng.tensor_add(TL, TL, AFB),
                  reads=(bAFB,), writes=(bTL,), tag="gaff")
            P.add("vector", lambda eng, ti=ti: eng.scalar_tensor_tensor(
                TL, NZT[:, ti, :], NSB, TL, ALU.mult, ALU.add),
                reads=(bNZT, bNSB), writes=(bTL,), tag="gnz")
            P.add("vector", lambda eng: eng.tensor_scalar_mul(
                TL, TL, 1.0 / 1.5),
                writes=(bTL,), tag="gdiv")
            def em_top(eng):
                eng.max(MXT, TL)
                return eng.drain()
            P.add("vector", em_top, reads=(bTL,), writes=(bMXT,), tag="gtop")
            P.add("vector", lambda eng: eng.tensor_tensor(
                MSK8, TL, bc_free(MXT[:, 1:2], 8), ALU.is_ge),
                reads=(bTL, bMXT), writes=(bMSK8,), tag="gmsk")
            P.add("vector", lambda eng: eng.tensor_scalar_mul(
                NM, MXT[:, 0:1], -1.0),
                reads=(bMXT,), writes=(bNM,), tag="gnm")
            P.add("scalar", lambda eng: eng.activation(
                E8, TL, ACTF.Exp, bias=NM),
                reads=(bTL, bNM), writes=(bE8,), tag="gexp")
            P.add("vector", lambda eng: eng.tensor_mul(E8, E8, MSK8),
                  reads=(bMSK8,), writes=(bE8,), tag="gem")
            P.add("vector", lambda eng: eng.reduce_sum(DEN, E8, axis=AX.X),
                  reads=(bE8,), writes=(bDEN,), tag="gden")
            def em_rcd(eng):
                eng.reciprocal(DEN, DEN)
                return eng.drain()
            P.add("vector", em_rcd, writes=(bDEN,), tag="grc")
            P.add("vector", lambda eng: eng.tensor_tensor(
                E8, E8, bc_free(DEN, 8), ALU.mult),
                reads=(bDEN,), writes=(bE8,), tag="gpr")
            P.add("vector", lambda eng: eng.tensor_tensor(
                E8, E8, bc_free(FB, 8), ALU.mult),
                reads=(bFB,), writes=(bE8,), tag="gprf")
            P.add("vector", lambda eng: eng.tensor_add(E8, E8, FIXG),
                reads=(bFIXG,), writes=(bE8,), tag="gprg")
            P.add("tensor", lambda eng: eng.transpose(
                ps[3][0:8, 0:128], E8, IDF),
                reads=(bE8, bIDF), writes=(bps[3],), tag="gptr")
            P.add("vector", lambda eng, ti=ti: eng.tensor_copy(
                PRS[:, ti * 128:(ti + 1) * 128], ps[3][0:8, 0:128]),
                reads=(bps[3],), writes=(bPRS,), tag="gpst")

    P.barrier()
    al.reset(markP)

    XMB1, bXMB1 = al.take((128, NCH, 1024), BF16, "XMB1")
    FIN, bFIN = al.take((128, NCH, 1024), F32, "FIN")
    MBA, bMBA = al.take((128, 24, 1024), BF16, "MBA")
    PB, bPB = al.take((128, 1024), F32, "PBs4")
    TMPX, bTMPX = al.take((128, 512), F32, "TMPXs4")
    TMPA, bTMPA = al.take((128, 512), F32, "TMPAs4")
    TMPB, bTMPB = al.take((128, 512), F32, "TMPBs4")
    TMPF, bTMPF = al.take((128, 1024), F32, "TMPFs4")
    W1S = [al.take((128, NCH, 128), BF16, f"W1S4_{i}") for i in range(2)]
    WSS = [al.take((128, 2, 128), BF16, f"WSS4_{i}") for i in range(2)]
    LNS4, bLNS4 = al.take((128, 512), F32, "LNS4")
    SQS4, bSQS4 = al.take((128, 512), F32, "SQS4")
    TM4, bTM4 = al.take((128, 512), F32, "TM4")

    _qc = [0]

    def wq():
        _qc[0] += 1
        return ("sync", "gpsimd")[_qc[0] % 2]

    def ln4_store(RESap, bRES, toff, tag):
        W = 512
        for ec in range(NCH):
            P.add("tensor", lambda eng, ec=ec, RESap=RESap: eng.matmul(
                ps[4][0:1, 0:W], ONES1, RESap[:, ec, :],
                start=(ec == 0), stop=(ec == NCH - 1)),
                reads=(bONES1, bRES), writes=(bps[4],), tag=tag + "lsum")
        for ec in range(NCH):
            P.add("scalar", lambda eng, ec=ec, RESap=RESap: eng.activation(
                SQS4, RESap[:, ec, :], ACTF.Square),
                reads=(bRES,), writes=(bSQS4,), tag=tag + "lsq")
            P.add("tensor", lambda eng, ec=ec: eng.matmul(
                ps[5][0:1, 0:W], ONES1, SQS4,
                start=(ec == 0), stop=(ec == NCH - 1)),
                reads=(bONES1, bSQS4), writes=(bps[5],), tag=tag + "lsumsq")
        P.add("vector", lambda eng: eng.tensor_scalar_mul(
            LNS4[0:1, :], ps[4][0:1, 0:W], 1.0 / 1024.0),
            reads=(bps[4],), writes=(bLNS4,), tag=tag + "lmean")
        P.add("vector", lambda eng: eng.tensor_scalar_mul(
            TM4[0:1, :], ps[5][0:1, 0:W], 1.0 / 1024.0),
            reads=(bps[5],), writes=(bTM4,), tag=tag + "lms")
        P.add("vector", lambda eng: eng.tensor_mul(
            SQS4[0:1, :], LNS4[0:1, :], LNS4[0:1, :]),
            reads=(bLNS4,), writes=(bSQS4,), tag=tag + "lm2")
        P.add("vector", lambda eng: eng.scalar_tensor_tensor(
            TM4[0:1, :], TM4[0:1, :], 1e-5, SQS4[0:1, :], ALU.add, ALU.subtract),
            reads=(bSQS4,), writes=(bTM4,), tag=tag + "lvar")
        P.add("scalar", lambda eng: eng.activation(
            TM4[0:1, :], TM4[0:1, :], ACTF.Sqrt),
            writes=(bTM4,), tag=tag + "lsqrt")
        def em_rc(eng):
            eng.reciprocal(TM4[0:1, :], TM4[0:1, :])
            return eng.drain()
        P.add("vector", em_rc, writes=(bTM4,), tag=tag + "lrecip")
        P.add("tensor", lambda eng: eng.matmul(
            ps[4][:, 0:W], ONESR[0:1, :], LNS4[0:1, :], start=True, stop=True),
            reads=(bONESR, bLNS4), writes=(bps[4],), tag=tag + "lbm")
        P.add("tensor", lambda eng: eng.matmul(
            ps[5][:, 0:W], ONESR[0:1, :], TM4[0:1, :], start=True, stop=True),
            reads=(bONESR, bTM4), writes=(bps[5],), tag=tag + "lbr")
        P.add("vector", lambda eng: eng.tensor_copy(LNS4, ps[4][:, 0:W]),
              reads=(bps[4],), writes=(bLNS4,), tag=tag + "lbmc")
        P.add("vector", lambda eng: eng.tensor_copy(TM4, ps[5][:, 0:W]),
              reads=(bps[5],), writes=(bTM4,), tag=tag + "lbrc")
        for ec in range(NCH):
            P.add("vector", lambda eng, ec=ec, RESap=RESap: eng.tensor_sub(
                RESap[:, ec, :], RESap[:, ec, :], LNS4),
                reads=(bLNS4,), writes=(bRES,), tag=tag + "lsub")
            P.add("vector", lambda eng, ec=ec, RESap=RESap: eng.tensor_mul(
                RESap[:, ec, :], RESap[:, ec, :], TM4),
                reads=(bTM4,), writes=(bRES,), tag=tag + "lmul")
        P.add("sync", lambda eng, toff=toff, RESap=RESap: eng.dma_start(
            out=rearr_cp(io["yT"].ap())[:, :, toff:toff + W], in_=RESap),
            reads=(bRES,), writes=(bY,), kind="dma", tag=tag + "st")

    def mix_evac(ec, tsub, bank, bcol, bufcol, first, tag):
        if first:
            P.add("vector", lambda eng, ec=ec, tsub=tsub, bank=bank,
                  bufcol=bufcol: eng.scalar_tensor_tensor(
                      TMPX, ps[bank][:, 0:512], bufcol,
                      PB[:, tsub * 512:(tsub + 1) * 512], ALU.add, ALU.mult),
                  reads=(bps[bank], bcol, bPB), writes=(bTMPX,), tag=tag + "mx")
        else:
            P.add("vector", lambda eng, ec=ec, tsub=tsub, bank=bank:
                  eng.tensor_tensor(
                      TMPX, ps[bank][:, 0:512],
                      PB[:, tsub * 512:(tsub + 1) * 512], ALU.mult),
                  reads=(bps[bank], bPB), writes=(bTMPX,), tag=tag + "mx")
        P.add("vector", lambda eng, ec=ec, tsub=tsub: eng.tensor_add(
            FIN[:, ec, tsub * 512:(tsub + 1) * 512],
            FIN[:, ec, tsub * 512:(tsub + 1) * 512], TMPX),
            reads=(bTMPX,), writes=(bFIN,), tag=tag + "fa")

    def expert_1h(k, w1d, bw1, w2d, bw2, nF, B1, bB1, b1base, B2, bB2, b2base,
                  ne1, nf2, act, tag):
        nh = nF // 2
        for half in range(2):
            for f in range(nh):
                F = half * nh + f
                a1, b1buf = W1S[F % 2]
                P.add(wq(), lambda eng, k=k, F=F, a1=a1, w1d=w1d: eng.dma_start(
                    out=a1, in_=w1d.ap().rearrange(
                        "(e c p) h -> p e c h", p=128, c=8)
                    [:, k, :, F * 128:(F + 1) * 128]),
                    reads=(bw1,), writes=(b1buf,), kind="dma", tag=tag + f"w1_{F}")
                for tsub in range(2):
                    for c in range(NCH):
                        P.add("tensor", lambda eng, c=c, tsub=tsub, a1=a1:
                              eng.matmul(
                                  ps[2][:, 0:512], a1[:, c, :],
                                  XMB1[:, c, tsub * 512:(tsub + 1) * 512],
                                  start=(c == 0), stop=(c == NCH - 1)),
                              reads=(b1buf, bXMB1), writes=(bps[2],), tag=tag + "h1")
                    bcol = B1[:, b1base + F:b1base + F + 1]
                    if act == "gelu":
                        P.add("scalar", lambda eng, f=f, tsub=tsub, bcol=bcol:
                              eng.activation(
                                  MBA[:, f, tsub * 512:(tsub + 1) * 512],
                                  ps[2][:, 0:512], ACTF.Gelu_apprx_tanh,
                                  bias=bcol),
                              reads=(bps[2], bB1), writes=(bMBA,), tag=tag + "ga")
                    else:
                        P.add("scalar", lambda eng, bcol=bcol: eng.activation(
                            TMPA, ps[2][:, 0:512], ACTF.Sigmoid, bias=bcol),
                            reads=(bps[2], bB1), writes=(bTMPA,), tag=tag + "sg")
                        P.add("scalar", lambda eng, bcol=bcol: eng.activation(
                            TMPB, ps[2][:, 0:512], ACTF.Identity, bias=bcol),
                            reads=(bps[2], bB1), writes=(bTMPB,), tag=tag + "id")
                        P.add("vector", lambda eng: eng.tensor_mul(
                            TMPB, TMPB, TMPA),
                            reads=(bTMPA,), writes=(bTMPB,), tag=tag + "m1")
                        P.add("vector", lambda eng, f=f, tsub=tsub:
                              eng.tensor_mul(
                                  MBA[:, f, tsub * 512:(tsub + 1) * 512],
                                  TMPB, TMPA),
                              reads=(bTMPB, bTMPA), writes=(bMBA,), tag=tag + "m2")
            for ec in range(NCH):
                for f2 in range(nh // 2):
                    a2, b2buf = WSS[f2 % 2]
                    P.add(wq(), lambda eng, k=k, ec=ec, f2=f2, half=half,
                          a2=a2, w2d=w2d, nF=nF: eng.dma_start(
                              out=a2, in_=w2d.ap().rearrange(
                                  "(e f p) d -> p e f d", p=128, f=nF)
                              [:, k, half * nh + 2 * f2:half * nh + 2 * f2 + 2,
                               ec * 128:(ec + 1) * 128]),
                          reads=(bw2,), writes=(b2buf,), kind="dma",
                          tag=tag + f"w2_{half}_{ec}_{f2}")
                    for fi in range(2):
                        f = 2 * f2 + fi
                        for tsub in range(2):
                            P.add("tensor", lambda eng, f=f, fi=fi, tsub=tsub,
                                  a2=a2, nh=nh: eng.matmul(
                                      ps[tsub][:, 0:512], a2[:, fi, :],
                                      MBA[:, f, tsub * 512:(tsub + 1) * 512],
                                      start=(f == 0), stop=(f == nh - 1)),
                                  reads=(b2buf, bMBA), writes=(bps[tsub],),
                                  tag=tag + "o")
                for tsub in range(2):
                    mix_evac(ec, tsub, tsub, bB2,
                             B2[:, b2base + ec:b2base + ec + 1],
                             half == 0, tag + "e")

    def expert_e1(k, tag):
        for f in range(16):
            a1, b1buf = W1S[f % 2]
            P.add(wq(), lambda eng, k=k, f=f, a1=a1: eng.dma_start(
                out=a1, in_=we11b.ap().rearrange(
                    "(e c p) h -> p e c h", p=128, c=8)
                [:, k, :, f * 128:(f + 1) * 128]),
                reads=(Buf_we11,), writes=(b1buf,), kind="dma", tag=tag + f"w1_{f}")
            for tsub in range(2):
                for c in range(NCH):
                    P.add("tensor", lambda eng, c=c, tsub=tsub, a1=a1:
                          eng.matmul(
                              ps[2][:, 0:512], a1[:, c, :],
                              XMB1[:, c, tsub * 512:(tsub + 1) * 512],
                              start=(c == 0), stop=(c == NCH - 1)),
                          reads=(b1buf, bXMB1), writes=(bps[2],), tag=tag + "h1")
                P.add("scalar", lambda eng, f=f, tsub=tsub, k=k: eng.activation(
                    MBA[:, f, tsub * 512:(tsub + 1) * 512], ps[2][:, 0:512],
                    ACTF.Silu, bias=B1E1[:, k * 16 + f:k * 16 + f + 1]),
                    reads=(bps[2], bB1E1), writes=(bMBA,), tag=tag + "a1")
        for gh in range(2):
            for g8 in range(8):
                g = gh * 8 + g8
                for f2 in range(8):
                    a2, b2buf = WSS[f2 % 2]
                    P.add(wq(), lambda eng, k=k, g=g, f2=f2, a2=a2: eng.dma_start(
                        out=a2, in_=we12b.ap().rearrange(
                            "(e f p) d -> p e f d", p=128, f=16)
                        [:, k, 2 * f2:2 * f2 + 2, g * 128:(g + 1) * 128]),
                        reads=(Buf_we12,), writes=(b2buf,), kind="dma",
                        tag=tag + f"w2_{g}_{f2}")
                    for fi in range(2):
                        f = 2 * f2 + fi
                        for tsub in range(2):
                            P.add("tensor", lambda eng, f=f, fi=fi, tsub=tsub,
                                  a2=a2: eng.matmul(
                                      ps[tsub][:, 0:512], a2[:, fi, :],
                                      MBA[:, f, tsub * 512:(tsub + 1) * 512],
                                      start=(f == 0), stop=(f == 15)),
                                  reads=(b2buf, bMBA), writes=(bps[tsub],),
                                  tag=tag + "h2")
                for tsub in range(2):
                    P.add("scalar", lambda eng, g8=g8, g=g, tsub=tsub, k=k:
                          eng.activation(
                              MBA[:, 16 + g8, tsub * 512:(tsub + 1) * 512],
                              ps[tsub][:, 0:512], ACTF.Silu,
                              bias=B2E1[:, k * 16 + g:k * 16 + g + 1]),
                          reads=(bps[tsub], bB2E1), writes=(bMBA,), tag=tag + "a2")
            for ec in range(NCH):
                for g4 in range(4):
                    a3, b3buf = WSS[g4 % 2]
                    P.add(wq(), lambda eng, k=k, ec=ec, g4=g4, gh=gh, a3=a3:
                          eng.dma_start(
                              out=a3, in_=we13b.ap().rearrange(
                                  "(e g p) d -> p e g d", p=128, g=16)
                              [:, k, gh * 8 + 2 * g4:gh * 8 + 2 * g4 + 2,
                               ec * 128:(ec + 1) * 128]),
                          reads=(Buf_we13,), writes=(b3buf,), kind="dma",
                          tag=tag + f"w3_{gh}_{ec}_{g4}")
                    for gi in range(2):
                        g8 = 2 * g4 + gi
                        for tsub in range(2):
                            P.add("tensor", lambda eng, g8=g8, gi=gi, tsub=tsub,
                                  a3=a3: eng.matmul(
                                      ps[tsub][:, 0:512], a3[:, gi, :],
                                      MBA[:, 16 + g8, tsub * 512:(tsub + 1) * 512],
                                      start=(g8 == 0), stop=(g8 == 7)),
                                  reads=(b3buf, bMBA), writes=(bps[tsub],),
                                  tag=tag + "o")
                for tsub in range(2):
                    mix_evac(ec, tsub, tsub, bB3E1,
                             B3E1[:, k * 8 + ec:k * 8 + ec + 1],
                             gh == 0, tag + "e")

    (we01b, Buf_we01), (we02b, Buf_we02) = WEB["we01"], WEB["we02"]
    (we11b, Buf_we11), (we12b, Buf_we12) = WEB["we11"], WEB["we12"]
    (we13b, Buf_we13) = WEB["we13"]
    (we21b, Buf_we21), (we22b, Buf_we22) = WEB["we21"], WEB["we22"]

    EXPERTS = [("e0", 0), ("e1", 0), ("e2", 0), ("e0", 1), ("e1", 1),
               ("e2", 1), ("e0", 2), ("e1", 2)]

    for tt in range(2):
        toff = tt * 1024
        P.add("sync", lambda eng, toff=toff: eng.dma_start(
            out=XMB1, in_=rearr_cp(xmbd.ap())[:, :, toff:toff + 1024]),
            reads=(bxmbd,), writes=(bXMB1,), kind="dma", tag=f"e4ldx{tt}")
        for ec in range(NCH):
            P.add("sync", lambda eng, ec=ec, toff=toff: eng.dma_start(
                out=TMPF, in_=rearr_cp(xmd.ap())[:, ec, toff:toff + 1024]),
                reads=(bxmd,), writes=(bTMPF,), kind="dma", tag=f"e4ldf{tt}_{ec}")
            P.add("vector", lambda eng, ec=ec: eng.tensor_scalar_mul(
                FIN[:, ec, :], TMPF, 0.1),
                reads=(bTMPF,), writes=(bFIN,), tag="fininit")
        for j, (typ, k) in enumerate(EXPERTS):
            for tsub in range(2):
                P.add("tensor", lambda eng, j=j, tt=tt, tsub=tsub: eng.matmul(
                    ps[3][0:1, 0:512], OH8[:, j:j + 1],
                    PRS[:, tt * 1024 + tsub * 512:tt * 1024 + (tsub + 1) * 512],
                    start=True, stop=True),
                    reads=(bOH8, bPRS), writes=(bps[3],), tag="pbx")
                P.add("vector", lambda eng: eng.tensor_copy(
                    R1[0:1, 0:512], ps[3][0:1, 0:512]),
                    reads=(bps[3],), writes=(bR1,), tag="pbxe")
                P.add("tensor", lambda eng: eng.matmul(
                    ps[3][:, 0:512], ONESR[0:1, :], R1[0:1, 0:512],
                    start=True, stop=True),
                    reads=(bONESR, bR1), writes=(bps[3],), tag="pbb")
                P.add("vector", lambda eng, tsub=tsub: eng.tensor_copy(
                    PB[:, tsub * 512:(tsub + 1) * 512], ps[3][:, 0:512]),
                    reads=(bps[3],), writes=(bPB,), tag="pbbe")
            tg = f"x{tt}{typ}{k}"
            if typ == "e0":
                expert_1h(k, we01b, Buf_we01, we02b, Buf_we02, 32,
                          B1E0, bB1E0, k * 32, B2E0, bB2E0, k * 8, 3, 32,
                          "gelu", tg)
            elif typ == "e2":
                expert_1h(k, we21b, Buf_we21, we22b, Buf_we22, 24,
                          B1E2, bB1E2, k * 24, B2E2, bB2E2, k * 8, 2, 24,
                          "ssig", tg)
            else:
                expert_e1(k, tg)
        for ec in range(NCH):
            P.add("sync", lambda eng, ec=ec, toff=toff: eng.dma_start(
                out=TMPF, in_=rearr_cp(x3d.ap())[:, ec, toff:toff + 1024]),
                reads=(bx3d,), writes=(bTMPF,), kind="dma", tag=f"e4ldr{tt}_{ec}")
            P.add("vector", lambda eng, ec=ec: eng.tensor_add(
                FIN[:, ec, :], FIN[:, ec, :], TMPF),
                reads=(bTMPF,), writes=(bFIN,), tag="finres")
        for hh in range(2):
            ln4_store(FIN[:, :, hh * 512:(hh + 1) * 512], bFIN,
                      toff + hh * 512, f"l4_{tt}_{hh}")

    P.barrier()
    al.reset(mark0)



def build_nc(n_stages=N_STAGES):
    nc = bass.Bass("TRN2", target_bir_lowering=False)
    io = {}
    io["xT"] = nc.dram_tensor("xT", [D, TC], F32, kind="ExternalInput")
    io["xhT"] = nc.dram_tensor("xhT", [D, HALO], F32, kind="ExternalInput")
    io["w_qkv"] = nc.dram_tensor("w_qkv", [D, 1536], F32, kind="ExternalInput")
    io["w_qkr"] = nc.dram_tensor("w_qkr", [D, 512], F32, kind="ExternalInput")
    io["w_vr"] = nc.dram_tensor("w_vr", [D, 512], F32, kind="ExternalInput")
    io["w_prj"] = nc.dram_tensor("w_prj", [D, D], F32, kind="ExternalInput")
    io["bmask"] = nc.dram_tensor("bmask", [128, 1152], F32, kind="ExternalInput")
    io["identb"] = nc.dram_tensor("identb", [128, 128], BF16, kind="ExternalInput")
    io["valid"] = nc.dram_tensor("valid", [1, HALO], F32, kind="ExternalInput")
    io["yT"] = nc.dram_tensor("yT", [D, TC], F32, kind="ExternalOutput")
    if n_stages >= 2:
        io["wq_pk"] = nc.dram_tensor("wq_pk", [D, 2048], F32, kind="ExternalInput")
        io["keysT"] = nc.dram_tensor("keysT", [2048, 256], F32, kind="ExternalInput")
        io["bn_gamma"] = nc.dram_tensor("bn_gamma", [128, 16], F32, kind="ExternalInput")
        io["bn_beta"] = nc.dram_tensor("bn_beta", [128, 16], F32, kind="ExternalInput")
        io["iota8"] = nc.dram_tensor("iota8", [128, 8], F32, kind="ExternalInput")
        io["identf"] = nc.dram_tensor("identf", [128, 128], F32, kind="ExternalInput")
        io["down_embed"] = nc.dram_tensor("down_embed", [65536, D], F32, kind="ExternalInput")
        io["up_embed"] = nc.dram_tensor("up_embed", [65536, D], F32, kind="ExternalInput")
        io["bnin"] = nc.dram_tensor("bnin", [128, 32], F32)
        io["bnout"] = nc.dram_tensor("bnout", [128, 32], F32)
    if n_stages >= 3:
        io["w_inproj"] = nc.dram_tensor("w_inproj", [1024, 4096], F32, kind="ExternalInput")
        io["w_xproj"] = nc.dram_tensor("w_xproj", [2048, 96], F32, kind="ExternalInput")
        io["w_dtproj"] = nc.dram_tensor("w_dtproj", [64, 2048], F32, kind="ExternalInput")
        io["w_outproj"] = nc.dram_tensor("w_outproj", [2048, 1024], F32, kind="ExternalInput")
        io["conv_w3"] = nc.dram_tensor("conv_w3", [2048, 4], F32, kind="ExternalInput")
        io["conv_b3"] = nc.dram_tensor("conv_b3", [128, 16], F32, kind="ExternalInput")
        io["dt_b3"] = nc.dram_tensor("dt_b3", [128, 16], F32, kind="ExternalInput")
        io["a_log3"] = nc.dram_tensor("a_log3", [2048, 16], F32, kind="ExternalInput")
        io["d_vec3"] = nc.dram_tensor("d_vec3", [128, 16], F32, kind="ExternalInput")
        io["n1w"] = nc.dram_tensor("n1w", [128, NCH], F32, kind="ExternalInput")
        io["n2w"] = nc.dram_tensor("n2w", [128, NCH], F32, kind="ExternalInput")
        io["hflag3"] = nc.dram_tensor("hflag3", [128, 256], F32, kind="ExternalInput")
        io["ohb"] = nc.dram_tensor("ohb", [96, 32], F32, kind="ExternalInput")
        io["ohc"] = nc.dram_tensor("ohc", [32, 16], F32, kind="ExternalInput")
        io["w_mlp1"] = nc.dram_tensor("w_mlp1", [1024, 4096], F32, kind="ExternalInput")
        io["w_mlp2"] = nc.dram_tensor("w_mlp2", [1024, 4096], F32, kind="ExternalInput")
        io["w_mlp3"] = nc.dram_tensor("w_mlp3", [4096, 1024], F32, kind="ExternalInput")
        io["hstin"] = nc.dram_tensor("hstin", [128, 256], F32)
        io["hstout"] = nc.dram_tensor("hstout", [256, 256], F32)
        io["b_hstin"] = Buf("hstin")
        io["b_hstout"] = Buf("hstout")
        io["hb3in"] = nc.dram_tensor("hb3in", [128, 48], F32)
        io["hb3out"] = nc.dram_tensor("hb3out", [256, 48], F32)
        io["b_hb3in"] = Buf("hb3in")
        io["b_hb3out"] = Buf("hb3out")
        io["w1b"] = nc.dram_tensor("w1b", [1024, 4096], BF16)
        io["w2b"] = nc.dram_tensor("w2b", [1024, 4096], BF16)
        io["w3b"] = nc.dram_tensor("w3b", [4096, 1024], BF16)
    if n_stages >= 4:
        for nm, shp in [("gate_w1", [1024, 1024]), ("gate_w2", [1024, 512]),
                        ("gate_w3", [512, 8]), ("gb1", [128, 8]),
                        ("lnsc", [128, 8]), ("lnof", [128, 8]),
                        ("gb2", [128, 4]), ("gb3", [8, 1]),
                        ("aff_w", [1024, 8]), ("ab4", [1, 8]), ("eu4", [1, 8]),
                        ("noiz", [2048, 8]), ("oh8", [8, 8]),
                        ("we01", [3072, 4096]), ("we02", [12288, 1024]),
                        ("we11", [3072, 2048]), ("we12", [6144, 2048]),
                        ("we13", [6144, 1024]), ("we21", [2048, 3072]),
                        ("we22", [6144, 1024]), ("b1e0", [128, 96]),
                        ("b2e0", [128, 24]), ("b1e1", [128, 48]),
                        ("b2e1", [128, 48]), ("b3e1", [128, 24]),
                        ("b1e2", [128, 48]), ("b2e2", [128, 16])]:
            io[nm] = nc.dram_tensor(nm, shp, F32, kind="ExternalInput")
        io["impd"] = nc.dram_tensor("impd", [1, 2048], F32)
        io["impall"] = nc.dram_tensor("impall", [8, 2048], F32)
        io["affin"] = nc.dram_tensor("affin", [128, 8], F32)
        io["affout"] = nc.dram_tensor("affout", [128, 8], F32)
        io["we01b"] = nc.dram_tensor("we01b", [3072, 4096], BF16)
        io["we02b"] = nc.dram_tensor("we02b", [12288, 1024], BF16)
        io["we11b"] = nc.dram_tensor("we11b", [3072, 2048], BF16)
        io["we12b"] = nc.dram_tensor("we12b", [6144, 2048], BF16)
        io["we13b"] = nc.dram_tensor("we13b", [6144, 1024], BF16)
        io["we21b"] = nc.dram_tensor("we21b", [2048, 3072], BF16)
        io["we22b"] = nc.dram_tensor("we22b", [6144, 1024], BF16)

    import os as _os
    _dbg = _os.environ.get("BASSDBG", "") == "1"
    _k = dict(kind="ExternalOutput") if _dbg else {}
    io["x1d"] = nc.dram_tensor("x1d", [D, TC], F32, **_k)
    io["x2d"] = nc.dram_tensor("x2d", [D, TC], F32, **_k)
    io["b_x2d"] = Buf("x2d")
    io["b_x1d"] = Buf("x1d")
    if n_stages >= 3:
        io["x3d"] = nc.dram_tensor("x3d", [D, TC], F32, **_k)
        io["b_x3d"] = Buf("x3d")
        io["yzd"] = nc.dram_tensor("yzd", [2048, TC], F32, **_k)
        io["cdsd"] = nc.dram_tensor("cdsd", [2048, TC], F32, **_k)
        io["zsd"] = nc.dram_tensor("zsd", [2048, TC], BF16, **_k)
        io["bcd"] = nc.dram_tensor("bcd", [32, TC], F32, **_k)
        io["hd3"] = nc.dram_tensor("hd3", [D, TC], F32, **_k)
        io["ubd"] = nc.dram_tensor("ubd", [2048, TC], F32, **_k)
    if n_stages >= 4:
        io["xmd"] = nc.dram_tensor("xmd", [D, TC], F32, **_k)
        io["xmbd"] = nc.dram_tensor("xmbd", [D, TC], BF16, **_k)
    io["kfd"] = nc.dram_tensor("kfd", [512, HALO + TC], BF16, **_k)
    io["krd"] = nc.dram_tensor("krd", [256, HALO + TC], BF16, **_k)
    io["vfd"] = nc.dram_tensor("vfd", [HALO + TC, 512], BF16, **_k)
    io["vrd"] = nc.dram_tensor("vrd", [HALO + TC, 512], BF16, **_k)
    io["qfd"] = nc.dram_tensor("qfd", [512, TC], BF16, **_k)
    io["qrd"] = nc.dram_tensor("qrd", [256, TC], BF16, **_k)
    io["kfq0"] = nc.dram_tensor("kfq0", [512, 1152], BF16, **_k)
    io["b_kfq0"] = Buf("kfq0")
    io["kfq1"] = nc.dram_tensor("kfq1", [512, 1152], BF16, **_k)
    io["b_kfq1"] = Buf("kfq1")
    io["vfq0"] = nc.dram_tensor("vfq0", [9 * 128, 512], BF16, **_k)
    io["b_vfq0"] = Buf("vfq0")
    io["qfq0"] = nc.dram_tensor("qfq0", [512, 128], BF16, **_k)
    io["b_qfq0"] = Buf("qfq0")
    io["bandhv"] = nc.dram_tensor("bandhv", [128, 256], F32, **_k)
    io["b_bandhv"] = Buf("bandhv")
    io["d_outf"] = nc.dram_tensor("d_outf", [D, 128], F32, **_k)
    io["b_d_outf"] = Buf("d_outf")
    io["d_var"] = nc.dram_tensor("d_var", [128, 16], F32, **_k)
    io["b_d_var"] = Buf("d_var")
    io["d_qbr"] = nc.dram_tensor("d_qbr", [128, 2048], BF16, **_k)
    io["b_d_qbr"] = Buf("d_qbr")
    io["d_qs0"] = nc.dram_tensor("d_qs0", [128, 32], F32, **_k)
    io["b_d_qs0"] = Buf("d_qs0")
    io["d_qs1"] = nc.dram_tensor("d_qs1", [128, 32], F32, **_k)
    io["b_d_qs1"] = Buf("d_qs1")
    io["d_qs"] = nc.dram_tensor("d_qs", [128, 64], F32, **_k)
    io["b_d_qs"] = Buf("d_qs")
    io["d_qb"] = nc.dram_tensor("d_qb", [128, 2048], BF16, **_k)
    io["b_d_qb"] = Buf("d_qb")
    io["d_idx8f"] = nc.dram_tensor("d_idx8f", [128, 64], F32, **_k)
    io["b_d_idx8f"] = Buf("d_idx8f")
    io["d_idxt"] = nc.dram_tensor("d_idxt", [64, 128], F32, **_k)
    io["b_d_idxt"] = Buf("d_idxt")
    io["d_tidx2"] = nc.dram_tensor("d_tidx2", [128, 64], I32, **_k)
    io["b_d_tidx2"] = Buf("d_tidx2")
    io["d_w8"] = nc.dram_tensor("d_w8", [128, 64], F32, **_k)
    io["b_d_w8"] = Buf("d_w8")
    io["d_wb2"] = nc.dram_tensor("d_wb2", [128, 128], BF16, **_k)
    io["b_d_wb2"] = Buf("d_wb2")
    io["d_hd"] = nc.dram_tensor("d_hd", [128, 64], F32, **_k)
    io["b_d_hd"] = Buf("d_hd")
    io["d_s8"] = nc.dram_tensor("d_s8", [128, 64], F32, **_k)
    io["b_d_s8"] = Buf("d_s8")
    io["ps00"] = nc.dram_tensor("ps00", [128, 128], F32, **_k)
    io["b_ps00"] = Buf("ps00")
    io["bandfar"] = nc.dram_tensor("bandfar", [128, 128], F32, **_k)
    io["b_bandfar"] = Buf("bandfar")
    io["bmask0"] = nc.dram_tensor("bmask0", [128, 1152], F32, **_k)
    io["b_bmask0"] = Buf("bmask0")
    io["valid0"] = nc.dram_tensor("valid0", [128, 1024], F32, **_k)
    io["b_valid0"] = Buf("valid0")
    io["bmask2"] = nc.dram_tensor("bmask2", [128, 1152], F32, **_k)
    io["b_bmask2"] = Buf("bmask2")
    io["band0"] = nc.dram_tensor("band0", [128, 1152], F32, **_k)
    io["b_band0"] = Buf("band0")
    io["band1"] = nc.dram_tensor("band1", [128, 1152], F32, **_k)
    io["b_band1"] = Buf("band1")
    io["pbf0"] = nc.dram_tensor("pbf0", [128, 1152], BF16, **_k)
    io["b_pbf0"] = Buf("pbf0")
    io["resd"] = nc.dram_tensor("resd", [D, TC], F32, **_k)
    io["b_resd"] = Buf("resd")
    io["osbd"] = nc.dram_tensor("osbd", [TC, D], BF16, **_k)
    io["b_osbd"] = Buf("osbd")

    P = Prog(nc)
    al = Alloc(nc, [16384, 16384, 12288, 4096])
    ps = [nc.alloc_psum_tensor(f"psf{i}", [128, 512], F32) for i in range(6)]
    bps = [Buf(f"psf{i}") for i in range(6)]
    pb = [nc.alloc_psum_tensor(f"psb{i}", [128, 1024], BF16) for i in range(2)]
    bpb = [Buf(f"psb{i}") for i in range(2)]

    io["b_yT"] = Buf("yT")
    build_stage1(P, al, io, ps, bps, pb, bpb)
    if n_stages >= 2:
        build_stage2(P, al, io, ps, bps, pb, bpb)
    if n_stages >= 3:
        build_stage3(P, al, io, ps, bps, pb, bpb)
    if n_stages >= 4:
        build_stage4(P, al, io, ps, bps, pb, bpb)

    bY = io["b_yT"]
    if n_stages < 4:
        src_final = {1: "x1d", 2: "x2d", 3: "x3d"}[n_stages]
        P.dma("sync", io["yT"].ap(), io[src_final].ap(),
              reads=(io["b_" + src_final],), writes=(bY,), tag="bypass")

    sems = {}
    stack = contextlib.ExitStack()
    for e in ENGS:
        sems["d_" + e] = stack.enter_context(nc.semaphore("d_" + e))
        sems["c_" + e] = stack.enter_context(nc.semaphore("c_" + e))
        sems["k_" + e] = stack.enter_context(nc.semaphore("k_" + e))
    sems["cc"] = stack.enter_context(nc.semaphore("cc"))
    P.build(sems)
    finals = P.finish_events()
    with nc.Block() as blk:
        @blk.sync
        def _(eng):
            P.emit_engine("sync", eng)
            for sem, val in finals:
                eng.wait_ge(sem, val)

        @blk.scalar
        def _(eng):
            P.emit_engine("scalar", eng)

        @blk.vector
        def _(eng):
            P.emit_engine("vector", eng)

        @blk.tensor
        def _(eng):
            P.emit_engine("tensor", eng)

        @blk.gpsimd
        def _(eng):
            P.emit_engine("gpsimd", eng)
    stack.close()
    return nc


def make_masks():
    q = np.arange(128)[:, None]
    k = np.arange(128)[None, :]
    causal = np.where(q - k >= 0, 0.0, NEG).astype(np.float32)
    far = np.where(q < k, 0.0, NEG).astype(np.float32)
    bmask = np.zeros((128, 1152), dtype=np.float32)
    bmask[:, 0:128] = far
    bmask[:, 1024:1152] = causal
    return bmask


def make_in_maps(inputs):
    x = np.asarray(inputs["L0_attn_x"])
    bmask = make_masks()
    ident = np.eye(128, dtype=np.float32).astype(ml_dtypes.bfloat16)
    in_maps = []
    for c in range(8):
        b, h = c // 2, c % 2
        xs = np.ascontiguousarray(x[b, h * TC:(h + 1) * TC].T)
        if h == 0:
            xh = np.zeros((D, HALO), dtype=np.float32)
            valid = np.full((1, HALO), NEG, dtype=np.float32)
        else:
            xh = np.ascontiguousarray(x[b, TC - HALO:TC].T)
            valid = np.zeros((1, HALO), dtype=np.float32)
        m = dict(
            xT=xs, xhT=xh,
            w_qkv=np.ascontiguousarray(inputs["L0_attn_w_qkv_full"]),
            w_qkr=np.ascontiguousarray(inputs["L0_attn_w_qk_red"]),
            w_vr=np.ascontiguousarray(inputs["L0_attn_w_v_red"]),
            w_prj=np.ascontiguousarray(inputs["L0_attn_w_proj"]),
            bmask=bmask, identb=ident, valid=valid,
        )
        if N_STAGES >= 2:
            keys = np.asarray(inputs["L0_ffn_keys"])
            kT = np.stack([np.ascontiguousarray(keys[g % 8, :, g // 8, :].T)
                           for g in range(16)])
            m["keysT"] = kT.reshape(2048, 256)
            m["wq_pk"] = np.ascontiguousarray(inputs["L0_ffn_w_q"])
            m["bn_gamma"] = np.ascontiguousarray(
                np.asarray(inputs["L0_ffn_bn_gamma"]).reshape(16, 128).T)
            m["bn_beta"] = np.ascontiguousarray(
                np.asarray(inputs["L0_ffn_bn_beta"]).reshape(16, 128).T)
            m["iota8"] = np.broadcast_to(np.arange(8, dtype=np.float32), (128, 8)).copy()
            m["identf"] = np.eye(128, dtype=np.float32)
            m["down_embed"] = np.ascontiguousarray(inputs["L0_ffn_down_embed"])
            m["up_embed"] = np.ascontiguousarray(inputs["L0_ffn_up_embed"])
        if N_STAGES >= 3:
            f32c = lambda a: np.ascontiguousarray(np.asarray(a, np.float32))
            col128 = lambda a: np.ascontiguousarray(
                np.asarray(a, np.float32).reshape(-1, 128).T)
            m["w_inproj"] = f32c(np.asarray(inputs["L1_attn_in_proj_w"]).T)
            m["w_xproj"] = f32c(np.asarray(inputs["L1_attn_x_proj_w"]).T)
            m["w_dtproj"] = f32c(np.asarray(inputs["L1_attn_dt_proj_w"]).T)
            m["w_outproj"] = f32c(np.asarray(inputs["L1_attn_out_proj_w"]).T)
            m["conv_w3"] = f32c(np.asarray(inputs["L1_attn_conv_w"]).reshape(2048, 4))
            m["conv_b3"] = col128(inputs["L1_attn_conv_b"])
            m["dt_b3"] = col128(inputs["L1_attn_dt_proj_b"])
            m["a_log3"] = f32c(inputs["L1_attn_A_log"])
            m["d_vec3"] = col128(inputs["L1_attn_D"])
            m["n1w"] = col128(inputs["L1_attn_norm1_w"])
            m["n2w"] = col128(inputs["L1_attn_norm2_w"])
            m["hflag3"] = np.full((128, 256), float(h == 1), dtype=np.float32)
            ohb = np.zeros((96, 32), dtype=np.float32)
            for j in range(16):
                ohb[64 + j, j] = 1.0
                ohb[80 + j, 16 + j] = 1.0
            m["ohb"] = ohb
            ohc = np.zeros((32, 16), dtype=np.float32)
            for j in range(16):
                ohc[16 + j, j] = 1.0
            m["ohc"] = ohc
            m["w_mlp1"] = f32c(np.asarray(inputs["L1_attn_w1"]).T)
            m["w_mlp2"] = f32c(np.asarray(inputs["L1_attn_w2"]).T)
            m["w_mlp3"] = f32c(np.asarray(inputs["L1_attn_w3"]).T)
        if N_STAGES >= 4:
            if not hasattr(make_in_maps, "_noise"):
                import jax as _jax
                with _jax.default_device(_jax.devices("cpu")[0]):
                    make_in_maps._noise = np.asarray(
                        _jax.random.normal(_jax.random.key(1), (B, T, 8)),
                        dtype=np.float32)
            noise = make_in_maps._noise
            colk = lambda a, w: np.ascontiguousarray(np.concatenate(
                [np.asarray(a, np.float32)[k].reshape(-1, 128).T
                 for k in range(np.asarray(a).shape[0])], axis=1))
            m["gate_w1"] = f32c(inputs["L1_ffn_gate_w1"])
            m["gate_w2"] = f32c(inputs["L1_ffn_gate_w2"])
            m["gate_w3"] = f32c(inputs["L1_ffn_gate_w3"])
            m["gb1"] = col128(inputs["L1_ffn_gate_b1"])
            m["lnsc"] = col128(inputs["L1_ffn_ln_scale"])
            m["lnof"] = col128(inputs["L1_ffn_ln_offset"])
            m["gb2"] = col128(inputs["L1_ffn_gate_b2"])
            m["gb3"] = f32c(np.asarray(inputs["L1_ffn_gate_b3"]).reshape(8, 1))
            m["aff_w"] = f32c(inputs["L1_ffn_aff_w"])
            m["ab4"] = f32c(np.asarray(inputs["L1_ffn_aff_b"]).reshape(1, 8))
            m["eu4"] = f32c(np.asarray(inputs["L1_ffn_expert_usage"]).reshape(1, 8))
            m["noiz"] = np.ascontiguousarray(noise[b, h * TC:(h + 1) * TC, :])
            m["oh8"] = np.eye(8, dtype=np.float32)
            m["we01"] = f32c(np.asarray(inputs["L1_ffn_e0_w1"]).reshape(3072, 4096))
            m["we02"] = f32c(np.asarray(inputs["L1_ffn_e0_w2"]).reshape(12288, 1024))
            m["we11"] = f32c(np.asarray(inputs["L1_ffn_e1_w1"]).reshape(3072, 2048))
            m["we12"] = f32c(np.asarray(inputs["L1_ffn_e1_w2"]).reshape(6144, 2048))
            m["we13"] = f32c(np.asarray(inputs["L1_ffn_e1_w3"]).reshape(6144, 1024))
            m["we21"] = f32c(np.asarray(inputs["L1_ffn_e2_w1"]).reshape(2048, 3072))
            m["we22"] = f32c(np.asarray(inputs["L1_ffn_e2_w2"]).reshape(6144, 1024))
            m["b1e0"] = colk(inputs["L1_ffn_e0_b1"], 32)
            m["b2e0"] = colk(inputs["L1_ffn_e0_b2"], 8)
            m["b1e1"] = colk(inputs["L1_ffn_e1_b1"], 16)
            m["b2e1"] = colk(inputs["L1_ffn_e1_b2"], 16)
            m["b3e1"] = colk(inputs["L1_ffn_e1_b3"], 8)
            m["b1e2"] = colk(inputs["L1_ffn_e2_b1"], 24)
            m["b2e2"] = colk(inputs["L1_ffn_e2_b2"], 8)
        in_maps.append(m)
    return in_maps


def kernel(**inputs):
    nc = build_nc()
    in_maps = make_in_maps(inputs)
    res = run_bass_kernel_spmd(nc, in_maps, core_ids=list(range(8)))
    out = np.empty((B, T, D), dtype=np.float32)
    for c in range(8):
        b, h = c // 2, c % 2
        out[b, h * TC:(h + 1) * TC] = res.results[c]["yT"].T
    kernel.last_results = res.results
    return out
